# version 11
# baseline (speedup 1.0000x reference)
import math
from contextlib import ExitStack

import numpy as np
import ml_dtypes

import concourse.bass as bass
import concourse.mybir as mybir
from concourse.bass_utils import run_bass_kernel_spmd
from concourse.bass_core_rust import IndirectOffsetOnAxis

F32 = mybir.dt.float32
BF16 = mybir.dt.bfloat16
I32 = mybir.dt.int32
AF = mybir.ActivationFunctionType
ALU = mybir.AluOpType
AX = mybir.AxisListType

NCORE = 8
TOK = 1024
NT = TOK // 128
KC = 16
D = 2048
NH = 16
NL = 64
HD = 128
NE = 16
EI = 1408
EIC = EI // 128
SI = 2816
SIC = SI // 128
SCALE = 1.0 / math.sqrt(HD)
EPS = 1e-5
CAP = 384
BIGF = float(1 << 28)

ENGINES = ("sync", "gpsimd", "tensor", "vector", "scalar")


class Sch:

    def __init__(self, nc, sems):
        self.nc = nc
        self.sems = sems
        self.cnt = {k: 0 for k in sems}
        self.ops = None

    def phase(self):
        self.ops = {e: [] for e in ENGINES}

    def flush(self):
        with self.nc.Block() as block:
            for ename in ENGINES:
                fns = self.ops[ename]
                if not fns:
                    continue
                getattr(block, ename)(lambda E, fns=fns: [f(E) for f in fns])
        self.ops = None

    def tok(self, name, k=1):
        self.cnt[name] += k
        return (name, self.cnt[name])

    def op(self, eng, emit, waits=(), inc=None, inc_k=1):
        t = self.tok(inc, inc_k) if inc else None
        sems = self.sems
        waits = tuple(w for w in waits if w is not None)

        def f(E, waits=waits, emit=emit, inc=inc, inc_k=inc_k):
            for (s, v) in waits:
                E.wait_ge(sems[s], v)
            inst = emit(E)
            if inc is not None:
                inst.then_inc(sems[inc], inc_k)
        self.ops[eng].append(f)
        return t

    def mm_group(self, mms, waits=(), inc="p"):
        t = self.tok(inc, 1)
        sems = self.sems
        waits = tuple(w for w in waits if w is not None)

        def f(E, waits=waits, mms=tuple(mms)):
            for (s, v) in waits:
                E.wait_ge(sems[s], v)
            n = len(mms)
            for i, (o, l, r) in enumerate(mms):
                inst = E.matmul(o, l, r, start=(i == 0), stop=(i == n - 1))
            inst.then_inc(sems[inc], 1)
        self.ops["tensor"].append(f)
        return t


class Banks:
    def __init__(self, tensors):
        self.t = tensors
        self.free_tok = [None] * len(tensors)
        self.i = 0

    def next(self):
        b = self.i
        self.i = (self.i + 1) % len(self.t)
        return b

    def waits(self, b):
        return [self.free_tok[b]] if self.free_tok[b] is not None else []

    def set_free(self, b, tokv):
        self.free_tok[b] = tokv


def build(n_experts=NE, debug=False, sparse=True):
    nc = bass.Bass("TRN2", target_bir_lowering=False)

    x_d = nc.dram_tensor("x", [TOK, D], F32, kind="ExternalInput")
    xp_d = nc.dram_tensor("xp", [128, D], F32, kind="ExternalInput")
    wk_d = nc.dram_tensor("wk", [D, D], F32, kind="ExternalInput")
    wv_d = nc.dram_tensor("wv", [D, D], F32, kind="ExternalInput")
    wg_d = nc.dram_tensor("wg", [D, NH * NL], F32, kind="ExternalInput")
    wp_d = nc.dram_tensor("wp", [D, D], F32, kind="ExternalInput")
    q_d = nc.dram_tensor("q", [128, NH, NL], F32, kind="ExternalInput")
    cos_d = nc.dram_tensor("cosr", [64, 128], F32, kind="ExternalInput")
    sin_d = nc.dram_tensor("sinr", [64, 128], F32, kind="ExternalInput")
    mask_d = nc.dram_tensor("maskr", [64, 128], F32, kind="ExternalInput")
    idb_d = nc.dram_tensor("idb", [128, 128], BF16, kind="ExternalInput")
    idf_d = nc.dram_tensor("idf", [128, 128], F32, kind="ExternalInput")
    gw_d = nc.dram_tensor("gwt", [D, NE], F32, kind="ExternalInput")
    up_d = nc.dram_tensor("upw", [NE, D, 2 * EI], F32, kind="ExternalInput")
    dn_d = nc.dram_tensor("dnw", [NE, EI, D], F32, kind="ExternalInput")
    sup_d = nc.dram_tensor("supw", [D, 2 * SI], F32, kind="ExternalInput")
    sdn_d = nc.dram_tensor("sdnw", [SI, D], F32, kind="ExternalInput")
    out_d = nc.dram_tensor("out", [TOK, D], F32, kind="ExternalOutput")
    if sparse:
        ltri_d = nc.dram_tensor("ltri", [128, 128], BF16, kind="ExternalInput")
        thr_d = nc.dram_tensor("thr", [128, NE], F32, kind="ExternalInput")
        ecrow_d = nc.dram_tensor("ecrow", [1, NE], F32, kind="ExternalInput")
        tokidf_d = nc.dram_tensor("tokidf", [128, NT], F32, kind="ExternalInput")
        bigf_d = nc.dram_tensor("bigf", [128, 384], F32, kind="ExternalInput")
        disp_d = nc.dram_tensor("disp", [NE * CAP, D], BF16, kind="Internal")
        aux_d = nc.dram_tensor("aux", [NE * CAP, 8], F32, kind="Internal")
        acc_d = [nc.dram_tensor(f"acc{k}", [TOK, D], BF16, kind="Internal") for k in range(4)]
    if debug:
        dbg_xmid = nc.dram_tensor("dbg_xmid", [D, TOK], F32, kind="ExternalOutput")
        dbg_attn = nc.dram_tensor("dbg_attn", [64, NH * HD], BF16, kind="ExternalOutput")
        dbg_cw = nc.dram_tensor("dbg_cw", [NE, TOK], F32, kind="ExternalOutput")
        dbg_ln2 = nc.dram_tensor("dbg_ln2", [D, TOK], BF16, kind="ExternalOutput")

    S0 = ExitStack()
    with S0:
        sems = {}
        for name in ("q0", "q1", "q2", "i", "x0", "x1", "o0", "o1", "g2", "g3", "g4", "c0", "c1", "z", "p", "v", "a"):
            sems[name] = S0.enter_context(nc.semaphore("s" + name))
        S = Sch(nc, sems)

        x_fm = S0.enter_context(nc.sbuf_tensor("x_fm", [128, KC, TOK], F32))
        ws = [S0.enter_context(nc.sbuf_tensor(f"ws{i}", [128, KC, 512], BF16)) for i in range(2)]
        attn_sb = S0.enter_context(nc.sbuf_tensor("attn_sb", [128, NH, HD], BF16))
        scr = [S0.enter_context(nc.sbuf_tensor(f"scr{i}", [128, 512], F32)) for i in range(2)]
        onesf = S0.enter_context(nc.sbuf_tensor("onesf", [128, 1], F32))
        epsc = S0.enter_context(nc.sbuf_tensor("epsc", [128, 1], F32))
        ones1 = S0.enter_context(nc.sbuf_tensor("ones1", [1, 128], F32))
        idb_sb = S0.enter_context(nc.sbuf_tensor("idb_sb", [128, 128], BF16))
        idf_sb = S0.enter_context(nc.sbuf_tensor("idf_sb", [128, 128], F32))
        if not sparse:
            cwT = S0.enter_context(nc.sbuf_tensor("cwT", [NE, TOK], F32))
        else:
            xg0 = S0.enter_context(nc.sbuf_tensor("xg0", [128, KC, CAP], BF16))
        gsmall = S0.enter_context(nc.sbuf_tensor("gsmall", [128, 12, 16], F32))
        ssmall = S0.enter_context(nc.sbuf_tensor("ssmall", [64, 4, 16], F32))

        rot_t = [S0.enter_context(nc.psum_tensor(f"rot{i}", [128, 512], F32)) for i in range(6)]
        PT6 = S0.enter_context(nc.psum_tensor("pt6", [128, 512], F32))
        PT7 = S0.enter_context(nc.psum_tensor("pt7", [128, 1024], BF16))
        rot = Banks(rot_t)
        pt6_free = [None]
        pt7_free = [None]
        scr_last_pe = [None]
        scr_last_dve = [None]

        ws_free = [None, None, None]
        ws_next = [0]
        ws_mod = [2]

        def stream_w(dram_ap, nchunks, ncols):
            p = ws_next[0]
            ws_next[0] = (ws_next[0] + 1) % ws_mod[0]
            t = S.op("gpsimd",
                     lambda E, d=dram_ap, p=p, nk=nchunks, ncol=ncols:
                     E.dma_start(ws[p][:, 0:nk, 0:ncol], d),
                     waits=[ws_free[p]], inc=f"q{p}", inc_k=16)
            return p, t

        def consume_w(p, tokv):
            ws_free[p] = tokv

        def stream_w2(dram_ap, nchunks, ncols):
            p = ws_next[0]
            ws_next[0] = (ws_next[0] + 1) % ws_mod[0]
            t = S.op("gpsimd",
                     lambda E, d=dram_ap, p=p, nk=nchunks, ncol=ncols:
                     E.dma_start(ws[p][:].rearrange("p a b -> p (a b)").rearrange("p (k m) -> p k m", m=ncol)[:, 0:nk, :], d),
                     waits=[ws_free[p]], inc=f"q{p}", inc_k=16)
            return p, t

        S.phase()
        stageS = ExitStack()
        stage = stageS.enter_context(nc.sbuf_tensor("stage", [128, 2, D], F32))

        tc_id = S.op("sync", lambda E: E.dma_start(idb_sb[:], idb_d[:, :]), inc="i", inc_k=16)
        tc_idf = S.op("sync", lambda E: E.dma_start(idf_sb[:], idf_d[:, :]), inc="i", inc_k=16)
        t_ones = S.op("vector", lambda E: E.memset(onesf[:], 1.0), inc="v")
        t_eps = S.op("vector", lambda E: E.memset(epsc[:], EPS), inc="v")
        t_ones1 = S.op("vector", lambda E: E.memset(ones1[:], 1.0), inc="v")

        wk_cb = []
        for cb in range(2):
            wk_cb.append(stream_w(wk_d[:, cb * 512:(cb + 1) * 512].rearrange("(kc p) n -> p kc n", p=128), KC, 512))

        stage_free = [None, None]
        for j in range(NT):
            sp = j % 2
            t_ld = S.op("sync", lambda E, j=j, sp=sp: E.dma_start(stage[:, sp, :], x_d[j * 128:(j + 1) * 128, :]),
                        waits=[stage_free[sp]], inc=f"x{sp}", inc_k=16)
            for g in range(4):
                tt = None
                for kc4 in range(4):
                    kc = g * 4 + kc4
                    w = ([t_ld, tc_idf] + ([pt6_free[0]] if pt6_free[0] is not None else [])) if kc4 == 0 else []
                    tt = S.op("tensor",
                              lambda E, sp=sp, kc=kc, kc4=kc4:
                              E.transpose(PT6[:, kc4 * 128:(kc4 + 1) * 128],
                                          stage[:, sp, kc * 128:(kc + 1) * 128], idf_sb[:]),
                              waits=w, inc=("p" if kc4 == 3 else None))
                t_ev = S.op("vector",
                            lambda E, j=j, g=g:
                            E.tensor_copy(x_fm[:, g * 4:(g + 1) * 4, j * 128:(j + 1) * 128],
                                          PT6[:].rearrange("p (c n) -> p c n", c=4)),
                            waits=[tt], inc="v")
                pt6_free[0] = t_ev
                if g == 3:
                    stage_free[sp] = t_ev
        S.flush()
        stageS.close()

        lnfS = ExitStack()
        ln_fm = lnfS.enter_context(nc.sbuf_tensor("ln_fm", [128, KC, TOK], BF16))
        preS = ExitStack()
        xp_sb = preS.enter_context(nc.sbuf_tensor("xp_sb", [128, D], F32))
        ptmp = preS.enter_context(nc.sbuf_tensor("ptmp", [128, D], F32))
        lnp_tm = preS.enter_context(nc.sbuf_tensor("lnp_tm", [128, D], BF16))
        lnp_fm = preS.enter_context(nc.sbuf_tensor("lnp_fm", [128, KC, 128], BF16))
        k_fm = preS.enter_context(nc.sbuf_tensor("k_fm", [128, NH, 128], BF16))
        kr = preS.enter_context(nc.sbuf_tensor("kr", [128, NH, 128], BF16))
        v_sb = preS.enter_context(nc.sbuf_tensor("v_sb", [128, NH, 128], BF16))
        v_tm = preS.enter_context(nc.sbuf_tensor("v_tm", [128, NH, 128], BF16))
        wT = preS.enter_context(nc.sbuf_tensor("wT", [128, NH, 64], BF16))
        q_sb = preS.enter_context(nc.sbuf_tensor("q_sb", [128, NH, NL], BF16))
        cos_sb = preS.enter_context(nc.sbuf_tensor("cos_sb", [64, 128], F32))
        sin_sb = preS.enter_context(nc.sbuf_tensor("sin_sb", [64, 128], F32))
        mask_sb = preS.enter_context(nc.sbuf_tensor("mask_sb", [64, 128], F32))
        pcol = preS.enter_context(nc.sbuf_tensor("pcol", [128, 8], F32))

        S.phase()
        tc_cos = S.op("sync", lambda E: E.dma_start(cos_sb[:], cos_d[:, :]), inc="i", inc_k=16)
        tc_sin = S.op("sync", lambda E: E.dma_start(sin_sb[:], sin_d[:, :]), inc="i", inc_k=16)
        tc_mask = S.op("sync", lambda E: E.dma_start(mask_sb[:], mask_d[:, :]), inc="i", inc_k=16)
        tc_q = S.op("gpsimd", lambda E: E.dma_start(q_sb[:], q_d[:, :, :]), inc="g2", inc_k=16)
        tc_xp = S.op("sync", lambda E: E.dma_start(xp_sb[:], xp_d[:, :]), inc="i", inc_k=16)
        tc_all = ("i", S.cnt["i"])

        _ln_n = [0]

        def emit_ln(dst_ln):
            last = None
            _ln_n[0] += 1
            lnS = ExitStack()
            stats = lnS.enter_context(nc.sbuf_tensor(f"stats{_ln_n[0]}", [1, 4, 512], F32))
            ab = lnS.enter_context(nc.sbuf_tensor(f"ab{_ln_n[0]}", [128, 2, 512], F32))
            for half in range(2):
                hsl = slice(half * 512, (half + 1) * 512)
                base_p = S.cnt["p"]
                base_q = base_p + 1
                sqt = [scr[0][:], scr[1][:], ab[:, 0, :], ab[:, 1, :]]
                sq_toks = []
                for kc in range(KC):
                    sp = kc % 4
                    if kc < 4:
                        w = [scr_last_pe[0]]
                    else:
                        w = [("p", base_q + (kc - 4) // 4 + 1)]
                    t_sq = S.op("vector",
                                lambda E, kc=kc, sp=sp, hsl=hsl:
                                E.tensor_mul(sqt[sp], x_fm[:, kc, hsl], x_fm[:, kc, hsl]),
                                waits=w, inc="v")
                    sq_toks.append((sp, t_sq))
                b_s, b_q = 4, 5
                mms = [(rot_t[b_s][0:1, :], onesf[:], x_fm[:, kc, hsl]) for kc in range(KC)]
                t_gs = S.mm_group(mms, waits=rot.waits(b_s) + [t_ones])
                for (s, v) in rot.waits(b_q):
                    S.ops["tensor"].append(lambda E, s=s, v=v: E.wait_ge(sems[s], v))
                t_gq = None
                for kc in range(KC):
                    sp, t_sq = sq_toks[kc]
                    if kc % 4 == 3:
                        t_gq = S.tok("p")

                    def femit(E, sp=sp, t_sq=t_sq, first=(kc == 0), last_=(kc == KC - 1), do_inc=(kc % 4 == 3)):
                        E.wait_ge(sems["v"], t_sq[1])
                        inst = E.matmul(rot_t[b_q][0:1, :], onesf[:], sqt[sp],
                                        start=first, stop=last_)
                        if do_inc:
                            inst.then_inc(sems["p"], 1)
                    S.ops["tensor"].append(femit)
                scr_last_pe[0] = t_gq
                t1 = S.op("vector", lambda E: E.tensor_scalar(stats[0:1, 0, :], rot_t[b_s][0:1, :], 1.0 / D, None, ALU.mult),
                          waits=[t_gs], inc="v")
                t2 = S.op("vector", lambda E: E.tensor_scalar(stats[0:1, 1, :], rot_t[b_q][0:1, :], 1.0 / D, None, ALU.mult),
                          waits=[t_gq], inc="v")
                t3 = S.op("vector", lambda E: E.tensor_mul(stats[0:1, 2, :], stats[0:1, 0, :], stats[0:1, 0, :]),
                          waits=[t1], inc="v")
                t4 = S.op("vector", lambda E: E.tensor_sub(stats[0:1, 1, :], stats[0:1, 1, :], stats[0:1, 2, :]),
                          waits=[t2, t3], inc="v")
                t5 = S.op("scalar", lambda E: E.activation(stats[0:1, 3, :], stats[0:1, 1, :], AF.Ln, bias=epsc[0:1, 0:1], scale=1.0),
                          waits=[t4, t_eps], inc="a")
                t6 = S.op("scalar", lambda E: E.activation(stats[0:1, 2, :], stats[0:1, 3, :], AF.Exp, bias=0.0, scale=-0.5),
                          waits=[t5], inc="a")
                t7 = S.op("vector", lambda E: E.scalar_tensor_tensor(stats[0:1, 3, :], stats[0:1, 0, :], -1.0, stats[0:1, 2, :], ALU.mult, ALU.mult),
                          waits=[t6], inc="v")
                b_a, b_b = 2, 3
                ta = S.mm_group([(rot_t[b_a][:, :], ones1[:], stats[0:1, 2, :])],
                                waits=rot.waits(b_a) + [t6, t_ones1])
                tb = S.mm_group([(rot_t[b_b][:, :], ones1[:], stats[0:1, 3, :])],
                                waits=rot.waits(b_b) + [t7])
                tae = S.op("vector", lambda E: E.tensor_copy(ab[:, 0, :], rot_t[b_a][:]), waits=[ta], inc="v")
                tbe = S.op("vector", lambda E: E.tensor_copy(ab[:, 1, :], rot_t[b_b][:]), waits=[tb], inc="v")
                rot.set_free(b_a, tae)
                rot.set_free(b_b, tbe)
                rot.set_free(b_s, t1)
                rot.set_free(b_q, t2)
                for kc in range(KC):
                    sp = kc % 2
                    S.op("vector",
                         lambda E, kc=kc, sp=sp, hsl=hsl:
                         E.tensor_mul(scr[sp][:], x_fm[:, kc, hsl], ab[:, 0, :]),
                         waits=([tae, tbe, scr_last_pe[0]] if kc < 2 else []), inc="v")
                    last = S.op("vector",
                                lambda E, kc=kc, sp=sp, hsl=hsl, dst=dst_ln:
                                E.tensor_add(dst[:, kc, hsl], scr[sp][:], ab[:, 1, :]),
                                inc="v")
            lnS.close()
            return last

        t_ln1 = emit_ln(ln_fm)

        tp1 = S.op("vector", lambda E: E.reduce_sum(pcol[:, 0:1], xp_sb[:], axis=AX.X), waits=[tc_all], inc="v")
        tp2 = S.op("vector", lambda E: E.tensor_scalar(pcol[:, 1:2], pcol[:, 0:1], 1.0 / D, None, ALU.mult), waits=[tp1], inc="v")
        tp3 = S.op("vector", lambda E: E.tensor_mul(ptmp[:], xp_sb[:], xp_sb[:]), waits=[tc_all], inc="v")
        tp4 = S.op("vector", lambda E: E.reduce_sum(pcol[:, 2:3], ptmp[:], axis=AX.X), waits=[tp3], inc="v")
        tp5 = S.op("vector", lambda E: E.tensor_scalar(pcol[:, 3:4], pcol[:, 2:3], 1.0 / D, None, ALU.mult), waits=[tp4], inc="v")
        tp6 = S.op("vector", lambda E: E.tensor_mul(pcol[:, 4:5], pcol[:, 1:2], pcol[:, 1:2]), waits=[tp2], inc="v")
        tp7 = S.op("vector", lambda E: E.tensor_sub(pcol[:, 5:6], pcol[:, 3:4], pcol[:, 4:5]), waits=[tp5, tp6], inc="v")
        tp8 = S.op("scalar", lambda E: E.activation(pcol[:, 6:7], pcol[:, 5:6], AF.Ln, bias=epsc[:, 0:1], scale=1.0), waits=[tp7, t_eps], inc="a")
        tp9 = S.op("scalar", lambda E: E.activation(pcol[:, 7:8], pcol[:, 6:7], AF.Exp, bias=0.0, scale=-0.5), waits=[tp8], inc="a")
        tp10 = S.op("vector", lambda E: E.scalar_tensor_tensor(pcol[:, 4:5], pcol[:, 1:2], -1.0, pcol[:, 7:8], ALU.mult, ALU.mult), waits=[tp9], inc="v")
        tp11 = S.op("vector", lambda E: E.tensor_scalar(lnp_tm[:], xp_sb[:], pcol[:, 7:8], pcol[:, 4:5], ALU.mult, ALU.add), waits=[tp10], inc="v")
        for g in range(2):
            tt = None
            for k8 in range(8):
                kc = g * 8 + k8
                w = ([tp11, tc_id] + ([pt7_free[0]] if pt7_free[0] is not None else [])) if k8 == 0 else []
                tt = S.op("tensor",
                          lambda E, kc=kc, k8=k8:
                          E.transpose(PT7[:, k8 * 128:(k8 + 1) * 128], lnp_tm[:, kc * 128:(kc + 1) * 128], idb_sb[:]),
                          waits=w, inc=("p" if k8 == 7 else None))
            t_ev = S.op("vector",
                        lambda E, g=g:
                        E.tensor_copy(lnp_fm[:, g * 8:(g + 1) * 8, :], PT7[:].rearrange("p (c n) -> p c n", c=8)),
                        waits=[tt], inc="v")
            pt7_free[0] = t_ev
        t_lnp = pt7_free[0]
        t_q = S.op("vector", lambda E: E.tensor_scalar(q_sb[:].rearrange("d h l -> d (h l)"), q_sb[:].rearrange("d h l -> d (h l)"), SCALE, None, ALU.mult),
                   waits=[tc_q], inc="v")
        S.flush()

        rp3S = ExitStack()
        rp3 = rp3S.enter_context(nc.sbuf_tensor("rp3", [64, D], F32))
        S.phase()
        kv_evac = {}
        for cb in range(4):
            if cb >= 2:
                wk_cb.append(stream_w(wk_d[:, cb * 512:(cb + 1) * 512].rearrange("(kc p) n -> p kc n", p=128), KC, 512))
            p, t_rdy = wk_cb[cb]
            b = rot.next()
            tg = None
            for oc4 in range(4):
                mms = [(rot_t[b][:, oc4 * 128:(oc4 + 1) * 128],
                        ws[p][:, kc, oc4 * 128:(oc4 + 1) * 128],
                        lnp_fm[:, kc, :]) for kc in range(KC)]
                w = ([t_rdy, t_lnp] + rot.waits(b)) if oc4 == 0 else []
                tg = S.mm_group(mms, waits=w)
            consume_w(p, tg)
            t_ev = S.op("vector",
                        lambda E, cb=cb, b=b:
                        E.tensor_copy(k_fm[:, cb * 4:(cb + 1) * 4, :], rot_t[b][:].rearrange("p (c n) -> p c n", c=4)),
                        waits=[tg], inc="v")
            rot.set_free(b, t_ev)
            kv_evac[("k", cb)] = t_ev
        for cb in range(4):
            p, t_rdy = stream_w(wv_d[:, cb * 512:(cb + 1) * 512].rearrange("(kc p) n -> p kc n", p=128), KC, 512)
            b = rot.next()
            tg = None
            for oc4 in range(4):
                mms = [(rot_t[b][:, oc4 * 128:(oc4 + 1) * 128],
                        ws[p][:, kc, oc4 * 128:(oc4 + 1) * 128],
                        lnp_fm[:, kc, :]) for kc in range(KC)]
                w = ([t_rdy] + rot.waits(b)) if oc4 == 0 else []
                tg = S.mm_group(mms, waits=w)
            consume_w(p, tg)
            t_ev = S.op("vector",
                        lambda E, cb=cb, b=b:
                        E.tensor_copy(v_sb[:, cb * 4:(cb + 1) * 4, :], rot_t[b][:].rearrange("p (c n) -> p c n", c=4)),
                        waits=[tg], inc="v")
            rot.set_free(b, t_ev)
            kv_evac[("v", cb)] = t_ev

        kw = [kv_evac[("k", cb)] for cb in range(4)] + [tc_all]
        cosb = bass.AP(cos_sb, 0, [[cos_sb.ap().ap[0][0], 64], [0, NH], [1, 128]])
        sinb = bass.AP(sin_sb, 0, [[sin_sb.ap().ap[0][0], 64], [0, NH], [1, 128]])

        def phv(t):
            return t.rearrange("p (h n) -> p h n", h=NH)
        R1 = phv(ptmp[0:64, :])
        R2 = phv(xp_sb[0:64, :])
        R3 = phv(rp3[0:64, :])
        KR1 = phv(lnp_tm[0:64, :])
        t_x1 = S.op("gpsimd", lambda E: E.dma_start(R2, k_fm[64:128, :, :]), waits=kw, inc="g2", inc_k=16)
        tr1 = S.op("vector", lambda E: E.tensor_mul(R1, k_fm[0:64, :, :], cosb), waits=kw, inc="v")
        tr2 = S.op("vector", lambda E: E.tensor_mul(R3, R2, sinb), waits=[t_x1], inc="v")
        tr3 = S.op("vector", lambda E: E.tensor_sub(kr[0:64, :, :], R1, R3), waits=[tr1, tr2], inc="v")
        tr4 = S.op("vector", lambda E: E.tensor_mul(R1, k_fm[0:64, :, :], sinb), waits=[tr3], inc="v")
        tr5 = S.op("vector", lambda E: E.tensor_mul(R3, R2, cosb), inc="v")
        tr6 = S.op("vector", lambda E: E.tensor_add(KR1, R1, R3), waits=[tr4, tr5], inc="v")
        t_kr1 = S.op("gpsimd", lambda E: E.dma_start(kr[64:128, :, :], KR1), waits=[tr6], inc="g2", inc_k=16)
        t_rope = tr3
        t_rope_dma = t_kr1

        we = ptmp
        smax_t = {}
        for h in range(NH):
            b = rot.next()
            tg = S.mm_group([(rot_t[b][0:64, 0:128], q_sb[:, h, :], kr[:, h, :])],
                            waits=([t_rope, t_rope_dma, t_q] if h == 0 else []) + rot.waits(b))
            t_s2 = S.op("vector",
                        lambda E, h=h, b=b: E.tensor_add(we[0:64, h * 128:(h + 1) * 128], rot_t[b][0:64, 0:128], mask_sb[:]),
                        waits=[tg, tc_all], inc="v")
            rot.set_free(b, t_s2)
            t_mx = S.op("vector",
                        lambda E, h=h: E.reduce_max(ssmall[:, 0, h:h + 1], we[0:64, h * 128:(h + 1) * 128], axis=AX.X, negate=True),
                        waits=[t_s2], inc="v")
            t_ex = S.op("scalar",
                        lambda E, h=h: E.activation(we[0:64, h * 128:(h + 1) * 128], we[0:64, h * 128:(h + 1) * 128], AF.Exp, bias=ssmall[:, 0, h:h + 1], scale=1.0),
                        waits=[t_mx], inc="a")
            t_sm = S.op("vector",
                        lambda E, h=h: E.reduce_sum(ssmall[:, 1, h:h + 1], we[0:64, h * 128:(h + 1) * 128], axis=AX.X),
                        waits=[t_ex], inc="v")
            t_rc = S.op("vector",
                        lambda E, h=h: E.reciprocal(ssmall[:, 2, h:h + 1], ssmall[:, 1, h:h + 1]),
                        waits=[t_sm], inc="v")
            smax_t[h] = (t_ex, t_rc)
        for h in range(NH):
            t_ex, t_rc = smax_t[h]
            tt = S.op("tensor",
                      lambda E, h=h: E.transpose(PT6[:, 0:64], we[0:64, h * 128:(h + 1) * 128], idf_sb[0:64, 0:64]),
                      waits=[t_ex] + ([pt6_free[0]] if pt6_free[0] is not None else []), inc="p")
            t_ev = S.op("vector",
                        lambda E, h=h: E.tensor_copy(wT[:, h, :], PT6[:, 0:64]),
                        waits=[tt], inc="v")
            pt6_free[0] = t_ev
            tt2 = S.op("tensor",
                       lambda E, h=h: E.transpose(PT7[:, 0:128], v_sb[:, h, :], idb_sb[:]),
                       waits=[kv_evac[("v", h // 4)]] + ([pt7_free[0]] if pt7_free[0] is not None else []), inc="p")
            t_ev2 = S.op("vector",
                         lambda E, h=h: E.tensor_copy(v_tm[:, h, :], PT7[:, 0:128]),
                         waits=[tt2], inc="v")
            pt7_free[0] = t_ev2
            b = rot.next()
            tg = S.mm_group([(rot_t[b][0:64, 0:128], wT[:, h, :], v_tm[:, h, :])],
                            waits=[t_ev, t_ev2] + rot.waits(b))
            t_at = S.op("vector",
                        lambda E, h=h, b=b: E.tensor_scalar(attn_sb[0:64, h, :], rot_t[b][0:64, 0:128], ssmall[:, 2, h:h + 1], None, ALU.mult),
                        waits=[tg, t_rc], inc="v")
            t_at2 = S.op("vector",
                         lambda E, h=h, b=b: E.tensor_scalar(attn_sb[64:128, h, :], rot_t[b][0:64, 0:128], ssmall[:, 2, h:h + 1], None, ALU.mult),
                         waits=[], inc="v")
            rot.set_free(b, t_at2)
        t_attn_all = ("v", S.cnt["v"])
        if debug:
            S.op("sync", lambda E: E.dma_start(dbg_attn.rearrange("p (h n) -> p h n", h=NH), attn_sb[0:64, :, :]),
                 waits=[t_attn_all], inc="i", inc_k=16)
        wg_pre = [stream_w(wg_d[:, og * 512:(og + 1) * 512].rearrange("(kc p) n -> p kc n", p=128), KC, 512)
                  for og in range(2)]
        S.flush()
        rp3S.close()
        preS.close()

        S.phase()
        abufS = ExitStack()
        gates_tm = abufS.enter_context(nc.sbuf_tensor("gates_tm", [128, 1024], BF16))
        g_sb = abufS.enter_context(nc.sbuf_tensor("g_sb", [128, 512], F32))
        if not sparse:
            crow = abufS.enter_context(nc.sbuf_tensor("crow", [1, TOK], F32))
            cwb = [abufS.enter_context(nc.sbuf_tensor(f"cwb{i}", [128, TOK], F32)) for i in range(2)]
        abuf0S = ExitStack()
        abuf = abuf0S.enter_context(nc.sbuf_tensor("abuf", [128, 24, 512], BF16))
        for half in range(2):
            hsl = slice(half * 512, (half + 1) * 512)
            wg_cb = wg_pre if half == 0 else [
                stream_w(wg_d[:, og * 512:(og + 1) * 512].rearrange("(kc p) n -> p kc n", p=128), KC, 512)
                for og in range(2)]
            t_gt_pe = None
            wg_last = [None, None]
            for tj in range(4):
                jt = half * 4 + tj
                for og in range(2):
                    p, t_rdy = wg_cb[og]
                    b = rot.next()
                    mms = [(rot_t[b][:, :], ln_fm[:, kc, jt * 128:(jt + 1) * 128], ws[p][:, kc, :])
                           for kc in range(KC)]
                    tg = S.mm_group(mms, waits=[t_rdy, t_ln1] + rot.waits(b))
                    wg_last[og] = tg
                    t_mx = S.op("vector",
                                lambda E, b=b: E.reduce_max(gsmall[:, 0, 0:8], rot_t[b][:].rearrange("p (h l) -> p h l", h=8), axis=AX.X, negate=True),
                                waits=[tg], inc="v")
                    t_sc = S.op("vector",
                                lambda E: E.tensor_scalar(gsmall[:, 1, 0:8], gsmall[:, 0, 0:8], SCALE, None, ALU.mult),
                                waits=[t_mx], inc="v")
                    t_exs = []
                    for h8 in range(8):
                        t_ex = S.op("scalar",
                                    lambda E, h8=h8, b=b: E.activation(g_sb[:, h8 * 64:(h8 + 1) * 64], rot_t[b][:, h8 * 64:(h8 + 1) * 64], AF.Exp, bias=gsmall[:, 1, h8:h8 + 1], scale=SCALE),
                                    waits=[t_sc] if h8 == 0 else [], inc="a")
                        t_exs.append(t_ex)
                    rot.set_free(b, t_exs[-1])
                    t_sm = S.op("vector",
                                lambda E: E.reduce_sum(gsmall[:, 2, 0:8], g_sb[:].rearrange("p (h l) -> p h l", h=8), axis=AX.X),
                                waits=[t_exs[-1]], inc="v")
                    t_rc = S.op("vector",
                                lambda E: E.reciprocal(gsmall[:, 3, 0:8], gsmall[:, 2, 0:8]),
                                waits=[t_sm], inc="v")
                    for h8 in range(8):
                        w = [t_rc] if h8 == 0 else []
                        if h8 == 0 and og == 0 and t_gt_pe is not None:
                            w = w + [t_gt_pe]
                        t_nm = S.op("vector",
                                    lambda E, h8=h8, og=og: E.tensor_scalar(gates_tm[:, og * 512 + h8 * 64:og * 512 + (h8 + 1) * 64], g_sb[:, h8 * 64:(h8 + 1) * 64], gsmall[:, 3, h8:h8 + 1], None, ALU.mult),
                                    waits=w, inc="v")
                for ch in range(8):
                    tt = S.op("tensor",
                              lambda E, ch=ch: E.transpose(PT7[:, 0:128], gates_tm[:, ch * 128:(ch + 1) * 128], idb_sb[:]),
                              waits=[t_nm] + ([pt7_free[0]] if pt7_free[0] is not None else []) if ch == 0 else ([pt7_free[0]] if pt7_free[0] is not None else []),
                              inc="p")
                    t_gt_pe = tt
                    t_ev = S.op("vector",
                                lambda E, ch=ch, tj=tj: E.tensor_copy(abuf[:, ch, tj * 128:(tj + 1) * 128], PT7[:, 0:128]),
                                waits=[tt], inc="v")
                    pt7_free[0] = t_ev
            for og in range(2):
                consume_w(wg_cb[og][0], wg_last[og])
            t_gfm = pt7_free[0]
            t_wtd = None
            for h in range(NH):
                b = rot.next()
                p0 = (h % 2) * 64
                tg = S.mm_group([(rot_t[b][:, :], attn_sb[p0:p0 + 64, h, :],
                                  abuf[p0:p0 + 64, h // 2, :])],
                                waits=[t_gfm, t_attn_all] + rot.waits(b))
                t_wtd = S.op("vector",
                             lambda E, h=h, b=b: E.tensor_copy(abuf[:, 8 + h, :], rot_t[b][:]),
                             waits=[tg], inc="v")
                rot.set_free(b, t_wtd)
            for cb in range(4):
                p, t_rdy = stream_w(wp_d[:, cb * 512:(cb + 1) * 512].rearrange("(kc p) n -> p kc n", p=128), KC, 512)
                tg = None
                for c in range(4):
                    b = rot.next()
                    mms = [(rot_t[b][:, :], ws[p][:, kc, c * 128:(c + 1) * 128], abuf[:, 8 + kc, :])
                           for kc in range(KC)]
                    w = ([t_rdy, t_wtd] if c == 0 else []) + rot.waits(b)
                    tg = S.mm_group(mms, waits=w)
                    t_ev = S.op("vector",
                                lambda E, cb=cb, c=c, b=b, hsl=hsl:
                                E.tensor_add(x_fm[:, cb * 4 + c, hsl], x_fm[:, cb * 4 + c, hsl], rot_t[b][:]),
                                waits=[tg], inc="v")
                    rot.set_free(b, t_ev)
                consume_w(p, tg)
        t_xmid = ("v", S.cnt["v"])
        if debug:
            S.op("sync", lambda E: E.dma_start(dbg_xmid.rearrange("(kc p) t -> p kc t", p=128), x_fm[:]),
                 waits=[t_xmid], inc="i", inc_k=16)
        S.flush()
        if sparse:
            abuf0S.close()

        S.phase()
        t_ln2 = emit_ln(ln_fm)
        if debug:
            S.op("sync", lambda E: E.dma_start(dbg_ln2.rearrange("(kc p) t -> p kc t", p=128), ln_fm[:]),
                 waits=[t_ln2], inc="i", inc_k=16)
        S.flush()

        fS = ExitStack()
        if sparse:
            lntm2 = fS.enter_context(nc.sbuf_tensor("lntm2", [128, 2, D], BF16))
            selb = fS.enter_context(nc.sbuf_tensor("selb", [128, NE], BF16))
            gs2 = fS.enter_context(nc.sbuf_tensor("gs2", [128, 6, NE], F32))
            becr = fS.enter_context(nc.sbuf_tensor("becr", [1, 2, NE], F32))
            skf = fS.enter_context(nc.sbuf_tensor("skf", [128, NT, 4], F32))
            cwf = fS.enter_context(nc.sbuf_tensor("cwf", [128, NT, 4], F32))
            auxsrc = fS.enter_context(nc.sbuf_tensor("auxsrc", [128, NT, 4, 8], F32))
            ski = fS.enter_context(nc.sbuf_tensor("ski", [128, NT, 4], I32))
            ltri_sb = fS.enter_context(nc.sbuf_tensor("ltri_sb", [128, 128], BF16))
            thr_sb = fS.enter_context(nc.sbuf_tensor("thr_sb", [128, NE], F32))
            tokidf_sb = fS.enter_context(nc.sbuf_tensor("tokidf_sb", [128, NT], F32))
            bigf_sb = fS.enter_context(nc.sbuf_tensor("bigf_sb", [128, 384], F32))
            onesb = fS.enter_context(nc.sbuf_tensor("onesb", [128, 1], BF16))
            cwT = fS.enter_context(nc.sbuf_tensor("cwT", [NE, TOK], F32))
            gwbuf = fS.enter_context(nc.sbuf_tensor("gwbuf", [128, KC, NE], BF16))
        S.phase()
        if sparse:
            tf1 = S.op("sync", lambda E: E.dma_start(ltri_sb[:], ltri_d[:, :]), inc="i", inc_k=16)
            tf2 = S.op("sync", lambda E: E.dma_start(thr_sb[:], thr_d[:, :]), inc="i", inc_k=16)
            tf3 = S.op("sync", lambda E: E.dma_start(becr[0:1, 0, :], ecrow_d[0:1, :]), inc="i", inc_k=16)
            tf4 = S.op("sync", lambda E: E.dma_start(tokidf_sb[:], tokidf_d[:, :]), inc="i", inc_k=16)
            tf5 = S.op("sync", lambda E: E.dma_start(bigf_sb[:], bigf_d[:, :]), inc="i", inc_k=16)
            tf_all = ("i", S.cnt["i"])
            t_onesb = S.op("vector", lambda E: E.memset(onesb[:], 1.0), inc="v")
            t_auxi = S.op("vector", lambda E: E.memset(auxsrc[:].rearrange("p a b c -> p (a b c)"), BIGF), inc="v")
            t_big = S.op("gpsimd",
                         lambda E: E.dma_start(aux_d[:, :].rearrange("(p a) o -> p (a o)", p=128), bigf_sb[:]),
                         waits=[tf_all], inc="g4", inc_k=16)
            lntm_free = [None, None]
            becr_tok = [None]
            pending_disp = [None]
            deferred = []
            r_bdisp = nc.gpsimd.alloc_register("r_bdisp")
            r_bacc = nc.gpsimd.alloc_register("r_bacc")
            S.op("gpsimd", lambda E: E.reg_mov(r_bdisp, NE * CAP - 1))
            S.op("gpsimd", lambda E: E.reg_mov(r_bacc, TOK - 1))
        scr0_war = [None]
        if sparse:
            t_gw = S.op("gpsimd", lambda E: E.dma_start(gwbuf[:], gw_d.rearrange("(kc p) n -> p kc n", p=128)),
                        inc="g2", inc_k=16)
            sharedS = ExitStack()
            asab = sharedS.enter_context(nc.sbuf_tensor("asab", [128, SIC, 512], BF16))

        def gen_shared():
            for half in range(2):
                hsl = slice(half * 512, (half + 1) * 512)
                tg_last = None
                for cb in range(11):
                    p, t_rdy = stream_w(sup_d[:, cb * 512:(cb + 1) * 512].rearrange("(kc p) n -> p kc n", p=128), KC, 512)
                    for pr in range(2):
                        bg = rot.next()
                        bu = rot.next()
                        mms_g = [(rot_t[bg][:, :], ws[p][:, kc, pr * 256:pr * 256 + 128], ln_fm[:, kc, hsl]) for kc in range(KC)]
                        mms_u = [(rot_t[bu][:, :], ws[p][:, kc, pr * 256 + 128:pr * 256 + 256], ln_fm[:, kc, hsl]) for kc in range(KC)]
                        wg_ = [t_rdy] if pr == 0 else []
                        ic = cb * 2 + pr
                        tgg = S.mm_group(mms_g, waits=wg_ + rot.waits(bg))
                        tgu = S.mm_group(mms_u, waits=rot.waits(bu))
                        tg_last = tgu
                        t_sg = S.op("scalar",
                                    lambda E, bg=bg: E.activation(scr[0][:], rot_t[bg][:], AF.Silu),
                                    waits=[tgg] + ([scr0_war[0]] if scr0_war[0] is not None else []), inc="a")
                        t_a = S.op("vector",
                                   lambda E, ic=ic, bu=bu: E.tensor_mul(asab[:, ic, :], scr[0][:], rot_t[bu][:]),
                                   waits=[t_sg, tgu], inc="v")
                        scr0_war[0] = t_a
                        rot.set_free(bg, t_a)
                        rot.set_free(bu, t_a)
                    consume_w(p, tg_last)
                    yield
                t_as = ("v", S.cnt["v"])
                tg = None
                for cb in range(8):
                    p, t_rdy = stream_w2(sdn_d[:, cb * 256:(cb + 1) * 256].rearrange("(kc p) n -> p kc n", p=128), SIC, 256)
                    wv22 = ws[p][:].rearrange("p a b -> p (a b)").rearrange("p (k m) -> p k m", m=256)
                    for c in range(2):
                        b = rot.next()
                        mms = [(rot_t[b][:, :], wv22[:, kc, c * 128:(c + 1) * 128], asab[:, kc, :])
                               for kc in range(SIC)]
                        w = ([t_rdy, t_as] if (c == 0 and cb == 0) else ([t_rdy] if c == 0 else [])) + rot.waits(b)
                        tg = S.mm_group(mms, waits=w)
                        t_ev = S.op("vector",
                                    lambda E, cb=cb, c=c, b=b, hsl=hsl:
                                    E.tensor_add(x_fm[:, cb * 2 + c, hsl], x_fm[:, cb * 2 + c, hsl], rot_t[b][:]),
                                    waits=[tg], inc="v")
                        rot.set_free(b, t_ev)
                    consume_w(p, tg)
                    yield

        if not sparse:
            p_gw, t_gw = stream_w(gw_d.rearrange("(kc p) n -> p kc n", p=128), KC, NE)
            pending_disp = [None]
        fout = {}

        def gen_tj():
          tg = None
          for tj in range(8):
            b = rot.next()
            gwsrc = gwbuf if sparse else ws[p_gw]
            mms = [(rot_t[b][:, 0:NE], ln_fm[:, kc, tj * 128:(tj + 1) * 128], gwsrc[:, kc, 0:NE])
                   for kc in range(KC)]
            tg = S.mm_group(mms, waits=[t_gw, t_ln2] + rot.waits(b))
            t1 = S.op("vector", lambda E, b=b: E.reduce_max(gsmall[:, 0, 0:1], rot_t[b][:, 0:NE], axis=AX.X, negate=True), waits=[tg], inc="v")
            t2 = S.op("scalar", lambda E, b=b: E.activation(gsmall[:, 3, 0:NE], rot_t[b][:, 0:NE], AF.Exp, bias=gsmall[:, 0, 0:1], scale=1.0), waits=[t1], inc="a")
            rot.set_free(b, t2)
            t6 = S.op("vector", lambda E: E.reduce_max(gsmall[:, 4, 0:4], gsmall[:, 3, 0:NE].rearrange("p (g e) -> p g e", g=4), axis=AX.X), waits=[t2], inc="v")
            tm1 = S.op("vector", lambda E: E.reduce_max(gsmall[:, 5, 0:1], gsmall[:, 4, 0:4], axis=AX.X), waits=[t6], inc="v")
            tm2 = S.op("vector", lambda E: E.tensor_scalar(gsmall[:, 5, 8:12], gsmall[:, 4, 0:4], gsmall[:, 5, 0:1], None, ALU.is_lt), waits=[tm1], inc="v")
            tm3 = S.op("vector", lambda E: E.tensor_mul(gsmall[:, 6, 0:4], gsmall[:, 4, 0:4], gsmall[:, 5, 8:12]), waits=[tm2], inc="v")
            tm4 = S.op("vector", lambda E: E.reduce_max(gsmall[:, 5, 1:2], gsmall[:, 6, 0:4], axis=AX.X), waits=[tm3], inc="v")
            t9 = S.op("vector", lambda E: E.tensor_scalar(gsmall[:, 6, 0:4], gsmall[:, 4, 0:4], gsmall[:, 5, 1:2], None, ALU.is_ge), waits=[tm4], inc="v")
            tA = None
            for gp in range(4):
                tA = S.op("vector", lambda E, gp=gp: E.tensor_scalar(gsmall[:, 7, gp * 4:(gp + 1) * 4], gsmall[:, 3, gp * 4:(gp + 1) * 4], gsmall[:, 6, gp:gp + 1], None, ALU.mult), waits=[t9] if gp == 0 else [], inc="v")
            tw = S.op("vector", lambda E: E.tensor_copy(gsmall[:, 8, 0:NE], gsmall[:, 7, 0:NE]), waits=[tA], inc="v")
            for step in range(3):
                tx = S.op("vector", lambda E, step=step: E.reduce_max(gsmall[:, 5, 2 + step:3 + step], gsmall[:, 8, 0:NE], axis=AX.X), waits=[tw], inc="v")
                ty = S.op("vector", lambda E, step=step: E.tensor_scalar(gsmall[:, 9, 0:NE], gsmall[:, 8, 0:NE], gsmall[:, 5, 2 + step:3 + step], None, ALU.is_lt), waits=[tx], inc="v")
                tw = S.op("vector", lambda E: E.tensor_mul(gsmall[:, 8, 0:NE], gsmall[:, 8, 0:NE], gsmall[:, 9, 0:NE]), waits=[ty], inc="v")
            tth = S.op("vector", lambda E: E.reduce_max(gsmall[:, 5, 5:6], gsmall[:, 8, 0:NE], axis=AX.X), waits=[tw], inc="v")
            tD = S.op("vector", lambda E: E.tensor_scalar(gsmall[:, 9, 0:NE], gsmall[:, 7, 0:NE], gsmall[:, 5, 5:6], None, ALU.is_ge), waits=[tth], inc="v")
            tE = S.op("vector", lambda E: E.tensor_mul(gsmall[:, 10, 0:NE], gsmall[:, 7, 0:NE], gsmall[:, 9, 0:NE]), waits=[tD], inc="v")
            tF = S.op("vector", lambda E: E.reduce_sum(gsmall[:, 11, 0:1], gsmall[:, 10, 0:NE], axis=AX.X), waits=[tE], inc="v")
            tG = S.op("vector", lambda E: E.tensor_scalar(gsmall[:, 11, 1:2], gsmall[:, 11, 0:1], 1e-20, None, ALU.add), waits=[tF], inc="v")
            tH = S.op("vector", lambda E: E.reciprocal(gsmall[:, 11, 2:3], gsmall[:, 11, 1:2]), waits=[tG], inc="v")
            tI = S.op("vector", lambda E: E.tensor_scalar(gsmall[:, 10, 0:NE], gsmall[:, 10, 0:NE], gsmall[:, 11, 2:3], None, ALU.mult), waits=[tH], inc="v")
            if sparse:
                yield
            tt = S.op("tensor", lambda E: E.transpose(PT6[0:NE, 0:128], gsmall[:, 10, 0:NE], idf_sb[:]),
                      waits=[tI] + ([pt6_free[0]] if pt6_free[0] is not None else []), inc="p")
            t_ev = S.op("vector", lambda E, tj=tj: E.tensor_copy(cwT[:, tj * 128:(tj + 1) * 128], PT6[0:NE, 0:128]), waits=[tt], inc="v")
            pt6_free[0] = t_ev
            if not sparse:
                continue
            t_selb = S.op("vector", lambda E: E.tensor_copy(selb[:], gsmall[:, 9, 0:NE]), waits=[tD], inc="v")
            bR = rot.next()
            tgR = S.mm_group([(rot_t[bR][:, 0:NE], ltri_sb[:], selb[:])],
                             waits=[t_selb, tf_all] + rot.waits(bR))
            bC = rot.next()
            tgC = S.mm_group([(rot_t[bC][0:1, 0:NE], onesb[:], selb[:])],
                             waits=[t_onesb] + rot.waits(bC))
            bB = rot.next()
            tgB = S.mm_group([(rot_t[bB][:, 0:NE], ones1[:], becr[0:1, 0, :])],
                             waits=([tf_all] if tj == 0 else [becr_tok[0]]) + rot.waits(bB))
            t_r = S.op("vector", lambda E, bR=bR: E.tensor_copy(gs2[:, 0, :], rot_t[bR][:, 0:NE]), waits=[tgR], inc="v")
            rot.set_free(bR, t_r)
            t_slot = S.op("vector", lambda E, bB=bB: E.tensor_add(gs2[:, 1, :], gs2[:, 0, :], rot_t[bB][:, 0:NE]), waits=[t_r, tgB], inc="v")
            rot.set_free(bB, t_slot)
            t_cnt = S.op("vector", lambda E, bC=bC: E.tensor_copy(becr[0:1, 1, :], rot_t[bC][0:1, 0:NE]), waits=[tgC], inc="v")
            rot.set_free(bC, t_cnt)
            t_bec = S.op("vector", lambda E: E.tensor_add(becr[0:1, 0, :], becr[0:1, 0, :], becr[0:1, 1, :]), waits=[t_cnt, tgB], inc="v")
            becr_tok[0] = t_bec
            t_ovf = S.op("vector", lambda E: E.tensor_tensor(gs2[:, 2, :], gs2[:, 1, :], thr_sb[:], ALU.is_ge), waits=[t_slot], inc="v")
            t_ns = S.op("vector", lambda E: E.tensor_scalar(gs2[:, 3, :], gsmall[:, 9, 0:NE], -1.0, 1.0, ALU.mult, ALU.add), inc="v")
            t_inv = S.op("vector", lambda E: E.tensor_add(gs2[:, 3, :], gs2[:, 3, :], gs2[:, 2, :]), waits=[t_ovf, t_ns], inc="v")
            t_sl2 = S.op("vector", lambda E: E.scalar_tensor_tensor(gs2[:, 4, :], gs2[:, 3, :], BIGF, gs2[:, 1, :], ALU.mult, ALU.add), waits=[t_inv], inc="v")
            t_last = t_sl2
            for k in range(4):
                t_mk = S.op("vector", lambda E, k=k: E.tensor_scalar(gs2[:, 5, :], gsmall[:, 7, 0:NE], gsmall[:, 5, 2 + k:3 + k], None, ALU.is_equal), waits=[t_last, tI] if k == 0 else [t_last], inc="v")
                t_sv = S.op("vector", lambda E: E.tensor_mul(gs2[:, 0, :], gs2[:, 5, :], gs2[:, 4, :]), waits=[t_mk], inc="v")
                t_sk = S.op("vector", lambda E, tj=tj, k=k: E.reduce_sum(skf[:, tj, k:k + 1], gs2[:, 0, :], axis=AX.X), waits=[t_sv], inc="v")
                t_cv = S.op("vector", lambda E: E.tensor_mul(gs2[:, 0, :], gs2[:, 5, :], gsmall[:, 10, 0:NE]), waits=[t_sk], inc="v")
                t_last = S.op("vector", lambda E, tj=tj, k=k: E.reduce_sum(cwf[:, tj, k:k + 1], gs2[:, 0, :], axis=AX.X), waits=[t_cv], inc="v")
            t_ski = S.op("vector", lambda E, tj=tj: E.tensor_copy(ski[:, tj, :], skf[:, tj, :]), waits=[t_last], inc="v")
            t_aux = None
            for k in range(4):
                S.op("vector", lambda E, tj=tj, k=k: E.tensor_copy(auxsrc[:, tj, k, 0:1], cwf[:, tj, k:k + 1]),
                     waits=[t_last, t_auxi] if k == 0 else [], inc="v")
                t_aux = S.op("vector", lambda E, tj=tj, k=k: E.tensor_copy(auxsrc[:, tj, k, 1 + k:2 + k], tokidf_sb[:, tj:tj + 1]),
                             inc="v")
            sp2 = tj % 2
            t_lev = None
            for g2 in range(2):
                tt2 = None
                for k8 in range(8):
                    kc = g2 * 8 + k8
                    w = ([pt7_free[0]] if pt7_free[0] is not None else []) if k8 == 0 else []
                    tt2 = S.op("tensor",
                               lambda E, kc=kc, k8=k8, tj=tj:
                               E.transpose(PT7[:, k8 * 128:(k8 + 1) * 128], ln_fm[:, kc, tj * 128:(tj + 1) * 128], idb_sb[:]),
                               waits=w, inc=("p" if k8 == 7 else None))
                wl = [lntm_free[sp2]] if (g2 == 0 and lntm_free[sp2] is not None) else []
                t_lev = S.op("vector",
                             lambda E, g2=g2, sp2=sp2:
                             E.tensor_copy(lntm2[:, sp2, g2 * 1024:(g2 + 1) * 1024], PT7[:]),
                             waits=[tt2] + wl, inc="v")
                pt7_free[0] = t_lev
            if pending_disp[0] is not None:
                pending_disp[0]()

            def mk_disp(tj=tj, sp2=sp2, t_ski=t_ski, t_aux=t_aux, t_lev=t_lev):
                def emit_():
                    t_xs = None
                    for k in range(4):
                        w = [t_ski, t_lev, t_big] if k == 0 else []
                        t_xs = S.op("gpsimd",
                                    lambda E, tj=tj, k=k, sp2=sp2:
                                    E.indirect_dma_start(disp_d[:, :], IndirectOffsetOnAxis(ski[:, tj, k:k + 1], 0),
                                                         lntm2[:, sp2, :], None,
                                                         bounds_check=r_bdisp, oob_is_err=False),
                                    waits=w, inc="g4", inc_k=16)
                    lntm_free[sp2] = t_xs
                    for k in range(4):
                        S.op("gpsimd",
                             lambda E, tj=tj, k=k:
                             E.indirect_dma_start(aux_d[:, :], IndirectOffsetOnAxis(ski[:, tj, k:k + 1], 0),
                                                  auxsrc[:, tj, k, :], None,
                                                  bounds_check=r_bdisp, oob_is_err=False),
                             waits=[t_aux, t_ski] if k == 0 else [], inc="g4", inc_k=16)
                return emit_
            pending_disp[0] = mk_disp()
            fout["tg"] = tg
            yield
          if pending_disp[0] is not None:
              pending_disp[0]()
              pending_disp[0] = None
          fout["tg"] = tg

        if sparse:
            gs_, gt_ = gen_shared(), gen_tj()
            done_s = done_t = False
            while not (done_s and done_t):
                if not done_s:
                    done_s = next(gs_, "END") == "END"
                if not done_t:
                    done_t = next(gt_, "END") == "END"
        else:
            for _ in gen_tj():
                pass
        tg = fout["tg"]
        if sparse:
            t_disp_pre = ("g4", S.cnt["g4"])
            t_xg0 = None
            for kc in range(KC):
                t_xg0 = S.op("sync",
                             lambda E, kc=kc:
                             E.dma_start(xg0[:, kc, :], disp_d[0:CAP, kc * 128:(kc + 1) * 128], transpose=True),
                             waits=[t_disp_pre] if kc == 0 else [], inc="x0", inc_k=16)
        t_cw = ("v", S.cnt["v"])
        if debug:
            S.op("sync", lambda E: E.dma_start(dbg_cw[:, :], cwT[:, :]), waits=[t_cw], inc="i", inc_k=16)
        if not sparse:
            consume_w(p_gw, tg)
        if sparse:
            t_disp_all = ("g4", S.cnt["g4"])
            S.flush()
            sharedS.close()
            fS.close()
            abufS.close()
            lnfS.close()

            expS = ExitStack()
            xg = [xg0, expS.enter_context(nc.sbuf_tensor("xg1", [128, KC, CAP], BF16))]
            asp = expS.enter_context(nc.sbuf_tensor("asp", [128, EIC, CAP], BF16))
            y_tm = [expS.enter_context(nc.sbuf_tensor(f"ytm{i}", [128, D], BF16)) for i in range(6)]
            auxt = [expS.enter_context(nc.sbuf_tensor(f"auxt{i}", [128, 3, 8], F32)) for i in range(2)]
            idxt = [expS.enter_context(nc.sbuf_tensor(f"idxt{i}", [128, 3, 4], I32)) for i in range(2)]
            ws.append(expS.enter_context(nc.sbuf_tensor("ws2", [128, KC, 512], BF16)))
            ws_mod[0] = 3
            rot.t.append(PT6)
            rot.free_tok.append(pt6_free[0])
            NTIL = CAP // 128
            S.phase()
            t_z0 = S.op("vector", lambda E: E.memset(y_tm[0][:], 0.0), inc="v")
            t_zero = None
            for k in range(4):
                for j in range(NT):
                    t_zero = S.op("sync",
                                  lambda E, k=k, j=j: E.dma_start(acc_d[k][j * 128:(j + 1) * 128, :], y_tm[0][:]),
                                  waits=[t_z0] if (k == 0 and j == 0) else [], inc="z", inc_k=16)
            xg_free = [None, None]
            cw_free = [None, None]
            y_free = [None] * 6
            abuf_free2 = [None]
            pending_scat = None
            for e in range(n_experts):
                pe = e % 2
                if e == 0:
                    t_xg = t_xg0
                else:
                    t_xg = None
                    for kc in range(KC):
                        w = ([xg_free[pe], t_disp_all] if kc == 0 else [])
                        t_xg = S.op("sync",
                                    lambda E, e=e, kc=kc, pe=pe:
                                    E.dma_start(xg[pe][:, kc, :], disp_d[e * CAP:(e + 1) * CAP, kc * 128:(kc + 1) * 128], transpose=True),
                                    waits=w, inc=f"x{pe}", inc_k=16)
                t_axl = S.op("sync",
                             lambda E, e=e, pe=pe:
                             E.dma_start(auxt[pe][:, :, :], aux_d[e * CAP:(e + 1) * CAP, :].rearrange("(i p) o -> p i o", p=128)),
                             waits=[cw_free[pe], t_disp_all] if cw_free[pe] is not None else [t_disp_all],
                             inc=f"c{pe}", inc_k=16)
                t_idx = S.op("vector", lambda E, pe=pe: E.tensor_copy(idxt[pe][:], auxt[pe][:, :, 1:5]), waits=[t_axl], inc="v")
                tg_last = None
                for cb in range(6):
                    ncol = 512 if cb < 5 else 256
                    npair = 2 if cb < 5 else 1
                    p, t_rdy = stream_w(up_d[e, :, cb * 512:cb * 512 + ncol].rearrange("(kc p) n -> p kc n", p=128), KC, ncol)
                    for pr in range(npair):
                        bg = rot.next()
                        bu = rot.next()
                        wg_ = [t_rdy] if pr == 0 else []
                        if cb == 0 and pr == 0:
                            wg_ = wg_ + [t_xg, abuf_free2[0]]
                        mms_g = [(rot.t[bg][:, 0:CAP], ws[p][:, kc, pr * 256:pr * 256 + 128], xg[pe][:, kc, :]) for kc in range(KC)]
                        mms_u = [(rot.t[bu][:, 0:CAP], ws[p][:, kc, pr * 256 + 128:pr * 256 + 256], xg[pe][:, kc, :]) for kc in range(KC)]
                        tgg = S.mm_group(mms_g, waits=wg_ + rot.waits(bg))
                        tgu = S.mm_group(mms_u, waits=rot.waits(bu))
                        tg_last = tgu
                        ic = cb * 2 + pr
                        t_sg = S.op("scalar",
                                    lambda E, bg=bg: E.activation(scr[0][:, 0:CAP], rot.t[bg][:, 0:CAP], AF.Silu),
                                    waits=[tgg] + ([scr0_war[0]] if scr0_war[0] is not None else []), inc="a")
                        t_a = S.op("vector",
                                   lambda E, ic=ic, bu=bu: E.tensor_mul(asp[:, ic, :], scr[0][:, 0:CAP], rot.t[bu][:, 0:CAP]),
                                   waits=[t_sg, tgu], inc="v")
                        scr0_war[0] = t_a
                        rot.set_free(bg, t_sg)
                        rot.set_free(bu, t_a)
                    consume_w(p, tg_last)
                xg_free[pe] = tg_last
                t_a_all = ("v", S.cnt["v"])
                t_ye = [None, None, None]
                for cb in range(4):
                    p, t_rdy = stream_w(dn_d[e, :, cb * 512:(cb + 1) * 512].rearrange("(kc p) n -> p kc n", p=128), EIC, 512)
                    tg = None
                    for i in range(NTIL):
                        b = rot.next()
                        mms = [(rot.t[b][:, :], asp[:, kc, i * 128:(i + 1) * 128], ws[p][:, kc, :])
                               for kc in range(EIC)]
                        w = ([t_rdy, t_a_all] if i == 0 else []) + rot.waits(b)
                        if cb == 0 and i == 0:
                            w = w + [t_idx]
                        tg = S.mm_group(mms, waits=w)
                        yt = pe * NTIL + i
                        wy = [y_free[yt]] if (cb == 0 and y_free[yt] is not None) else []
                        t_ye[i] = S.op("vector",
                                       lambda E, i=i, cb=cb, b=b, pe=pe, yt=yt:
                                       E.tensor_scalar(y_tm[yt][:, cb * 512:(cb + 1) * 512], rot.t[b][:], auxt[pe][:, i, 0:1], None, ALU.mult),
                                       waits=[tg] + wy, inc="v")
                        rot.set_free(b, t_ye[i])
                    consume_w(p, tg)
                abuf_free2[0] = tg
                if pending_scat is not None:
                    pending_scat()
                    pending_scat = None

                def mk_scat(e=e, pe=pe, t_ye=tuple(t_ye), t_idx=t_idx, first=(e == 0)):
                    def emit_():
                        for i in range(NTIL):
                            t_s = None
                            for k in range(4):
                                w = [t_ye[i], t_idx]
                                if first and i == 0 and k == 0:
                                    w = w + [t_zero]
                                t_s = S.op("gpsimd",
                                           lambda E, i=i, k=k, pe=pe:
                                           E.indirect_dma_start(acc_d[k][:, :], IndirectOffsetOnAxis(idxt[pe][:, i, k:k + 1], 0),
                                                                y_tm[pe * NTIL + i][:, :], None,
                                                                bounds_check=r_bacc, oob_is_err=False),
                                           waits=w, inc="g3", inc_k=16)
                            y_free[pe * NTIL + i] = t_s
                        cw_free[pe] = t_s
                    return emit_
                pending_scat = mk_scat()
            if pending_scat is not None:
                pending_scat()
                pending_scat = None
            t_scat_all = ("g3", S.cnt["g3"])
            S.flush()
            expS.close()
            ws.pop()
            ws_mod[0] = 2
            rot.t.pop()
            pt6_free[0] = rot.free_tok.pop()
            rot.i = rot.i % 6

            mergeS = ExitStack()
            macc = [mergeS.enter_context(nc.sbuf_tensor(f"macc{i}", [128, D], BF16)) for i in range(2)]
            msum = mergeS.enter_context(nc.sbuf_tensor("msum", [128, D], F32))
            S.phase()
            m_free = [None, None]
            mtb = [PT6, rot_t[0]]
            mtb_free = [pt6_free[0], rot.free_tok[0]]
            msum_pe = [None]
            for tj in range(NT):
                tl = [None] * 4
                for k in range(4):
                    mp = k % 2
                    w = ([t_scat_all] if tj == 0 and k < 2 else [])
                    if m_free[mp] is not None:
                        w = w + [m_free[mp]]
                    tl[k] = S.op("sync",
                                 lambda E, k=k, tj=tj, mp=mp: E.dma_start(macc[mp][:], acc_d[k][tj * 128:(tj + 1) * 128, :]),
                                 waits=w, inc=f"c{mp}", inc_k=16)
                    if k == 1:
                        s1w = [tl[0], tl[1]]
                        if msum_pe[0] is not None:
                            s1w = s1w + [msum_pe[0]]
                        s1 = S.op("vector", lambda E: E.tensor_add(msum[:], macc[0][:], macc[1][:]), waits=s1w, inc="v")
                        m_free[0] = s1
                        m_free[1] = s1
                s2 = S.op("vector", lambda E: E.tensor_add(msum[:], msum[:], macc[0][:]), waits=[tl[2], s1], inc="v")
                s3 = S.op("vector", lambda E: E.tensor_add(msum[:], msum[:], macc[1][:]), waits=[tl[3], s2], inc="v")
                m_free[0] = s2
                m_free[1] = s3
                for g in range(4):
                    bk2 = (tj * 4 + g) % 2
                    tt = None
                    for kc4 in range(4):
                        kc = g * 4 + kc4
                        w = ([s3] + ([mtb_free[bk2]] if mtb_free[bk2] is not None else [])) if kc4 == 0 else []
                        tt = S.op("tensor",
                                  lambda E, kc=kc, kc4=kc4, tj=tj, bk2=bk2:
                                  E.transpose(mtb[bk2][:, kc4 * 128:(kc4 + 1) * 128], msum[:, kc * 128:(kc + 1) * 128], idf_sb[:]),
                                  waits=w, inc=("p" if kc4 == 3 else None))
                    t_ev = S.op("vector",
                                lambda E, g=g, tj=tj, bk2=bk2:
                                E.tensor_add(x_fm[:, g * 4:(g + 1) * 4, tj * 128:(tj + 1) * 128],
                                             x_fm[:, g * 4:(g + 1) * 4, tj * 128:(tj + 1) * 128],
                                             mtb[bk2][:].rearrange("p (c n) -> p c n", c=4)),
                                waits=[tt], inc="v")
                    mtb_free[bk2] = t_ev
                msum_pe[0] = tt
            t_final = ("v", S.cnt["v"])
            S.flush()
            mergeS.close()
        else:
            S.flush()
            S.phase()
            cwb_ready = [None, None]
            last_bcast = [None]
            abuf_free = [None]
            scr0_war = [None]
            t_ev = None

            def emit_cwb(e):
                par = e % 2
                t_row = S.op("gpsimd", lambda E, e=e: E.dma_start(crow[0:1, :], cwT[e:e + 1, :]),
                             waits=[t_cw, cwb_ready[par], last_bcast[0]],
                             inc="g2", inc_k=16)
                toks = []
                for half in range(2):
                    b = rot.next()
                    tgb = S.mm_group([(rot_t[b][:, :], ones1[:], crow[0:1, half * 512:(half + 1) * 512])],
                                     waits=[t_row] + rot.waits(b))
                    last_bcast[0] = tgb
                    t_evb = S.op("vector",
                                 lambda E, par=par, half=half, b=b:
                                 E.tensor_copy(cwb[par][:, half * 512:(half + 1) * 512], rot_t[b][:]),
                                 waits=[tgb], inc="v")
                    rot.set_free(b, t_evb)
                    toks.append(t_evb)
                cwb_ready[par] = toks[-1]
                return toks[-1]

            for e in range(n_experts):
                t_cwb = emit_cwb(e)
                par = e % 2
                tg_last = None
                for cb in range(6):
                    ncol = 512 if cb < 5 else 256
                    npair = 2 if cb < 5 else 1
                    p, t_rdy = stream_w(up_d[e, :, cb * 512:cb * 512 + ncol].rearrange("(kc p) n -> p kc n", p=128), KC, ncol)
                    for pr in range(npair):
                        for half in range(2):
                            bg = rot.next()
                            bu = rot.next()
                            hsl = slice(half * 512, (half + 1) * 512)
                            wg_ = [t_rdy] if (pr == 0 and half == 0) else []
                            mms_g = [(rot_t[bg][:, :], ws[p][:, kc, pr * 256:pr * 256 + 128], ln_fm[:, kc, hsl]) for kc in range(KC)]
                            mms_u = [(rot_t[bu][:, :], ws[p][:, kc, pr * 256 + 128:pr * 256 + 256], ln_fm[:, kc, hsl]) for kc in range(KC)]
                            tgg = S.mm_group(mms_g, waits=wg_ + rot.waits(bg))
                            tgu = S.mm_group(mms_u, waits=rot.waits(bu))
                            tg_last = tgu
                            ic = cb * 2 + pr
                            aw = [abuf_free[0]] if (ic == 0 and half == 0) else []
                            aw = aw + ([scr0_war[0]] if scr0_war[0] is not None else [])
                            t_sg = S.op("scalar",
                                        lambda E, bg=bg: E.activation(scr[0][:], rot_t[bg][:], AF.Silu),
                                        waits=[tgg] + aw, inc="a")
                            t_t = S.op("vector",
                                       lambda E, bu=bu: E.tensor_mul(scr[1][:], scr[0][:], rot_t[bu][:]),
                                       waits=[t_sg, tgu] + ([t_cwb] if (ic == 0 and half == 0) else []), inc="v")
                            scr0_war[0] = t_t
                            t_a = S.op("vector",
                                       lambda E, ic=ic, half=half, par=par, hsl=hsl:
                                       E.tensor_mul(abuf[:, 2 * ic + half, :], scr[1][:], cwb[par][:, hsl]),
                                       inc="v")
                            rot.set_free(bg, t_a)
                            rot.set_free(bu, t_a)
                    consume_w(p, tg_last)
                t_a_all = ("v", S.cnt["v"])
                for cb in range(4):
                    p, t_rdy = stream_w(dn_d[e, :, cb * 512:(cb + 1) * 512].rearrange("(kc p) n -> p kc n", p=128), EIC, 512)
                    tg = None
                    for c in range(4):
                        for half in range(2):
                            b = rot.next()
                            hsl = slice(half * 512, (half + 1) * 512)
                            mms = [(rot_t[b][:, :], ws[p][:, kc, c * 128:(c + 1) * 128],
                                    abuf[:, 2 * kc + half, :]) for kc in range(EIC)]
                            w = ([t_rdy, t_a_all] if (c == 0 and half == 0) else []) + rot.waits(b)
                            tg = S.mm_group(mms, waits=w)
                            t_ev = S.op("vector",
                                        lambda E, cb=cb, c=c, b=b, hsl=hsl:
                                        E.tensor_add(x_fm[:, cb * 4 + c, hsl], x_fm[:, cb * 4 + c, hsl], rot_t[b][:]),
                                        waits=[tg], inc="v")
                            rot.set_free(b, t_ev)
                    consume_w(p, tg)
                abuf_free[0] = tg

            for half in range(2):
                hsl = slice(half * 512, (half + 1) * 512)
                tg_last = None
                for cb in range(11):
                    p, t_rdy = stream_w(sup_d[:, cb * 512:(cb + 1) * 512].rearrange("(kc p) n -> p kc n", p=128), KC, 512)
                    for pr in range(2):
                        bg = rot.next()
                        bu = rot.next()
                        mms_g = [(rot_t[bg][:, :], ws[p][:, kc, pr * 256:pr * 256 + 128], ln_fm[:, kc, hsl]) for kc in range(KC)]
                        mms_u = [(rot_t[bu][:, :], ws[p][:, kc, pr * 256 + 128:pr * 256 + 256], ln_fm[:, kc, hsl]) for kc in range(KC)]
                        wg_ = [t_rdy] if pr == 0 else []
                        if cb == 0 and pr == 0:
                            wg_ = wg_ + [abuf_free[0]]
                        ic = cb * 2 + pr
                        tgg = S.mm_group(mms_g, waits=wg_ + rot.waits(bg))
                        tgu = S.mm_group(mms_u, waits=rot.waits(bu))
                        tg_last = tgu
                        t_sg = S.op("scalar",
                                    lambda E, bg=bg: E.activation(scr[0][:], rot_t[bg][:], AF.Silu),
                                    waits=[tgg] + ([scr0_war[0]] if scr0_war[0] is not None else []), inc="a")
                        t_a = S.op("vector",
                                   lambda E, ic=ic, bu=bu: E.tensor_mul(abuf[:, ic, :], scr[0][:], rot_t[bu][:]),
                                   waits=[t_sg, tgu], inc="v")
                        scr0_war[0] = t_a
                        rot.set_free(bg, t_a)
                        rot.set_free(bu, t_a)
                    consume_w(p, tg_last)
                t_as = ("v", S.cnt["v"])
                tg = None
                for cb in range(8):
                    p, t_rdy = stream_w2(sdn_d[:, cb * 256:(cb + 1) * 256].rearrange("(kc p) n -> p kc n", p=128), SIC, 256)
                    wv22 = ws[p][:].rearrange("p a b -> p (a b)").rearrange("p (k m) -> p k m", m=256)
                    for c in range(2):
                        b = rot.next()
                        mms = [(rot_t[b][:, :], wv22[:, kc, c * 128:(c + 1) * 128], abuf[:, kc, :])
                               for kc in range(SIC)]
                        w = ([t_rdy, t_as] if (c == 0 and cb == 0) else ([t_rdy] if c == 0 else [])) + rot.waits(b)
                        tg = S.mm_group(mms, waits=w)
                        t_ev = S.op("vector",
                                    lambda E, cb=cb, c=c, b=b, hsl=hsl:
                                    E.tensor_add(x_fm[:, cb * 2 + c, hsl], x_fm[:, cb * 2 + c, hsl], rot_t[b][:]),
                                    waits=[tg], inc="v")
                        rot.set_free(b, t_ev)
                    consume_w(p, tg)
                abuf_free[0] = tg
            t_final = ("v", S.cnt["v"])
            S.flush()
            abuf0S.close()
            abufS.close()
            lnfS.close()

        S.phase()
        outS = ExitStack()
        xout = [outS.enter_context(nc.sbuf_tensor(f"xout{i}", [128, D], F32)) for i in range(2)]
        xout_free = [None, None]
        tb = [PT6, rot_t[0]]
        tb_free = [mtb_free[0], mtb_free[1]] if sparse else [pt6_free[0], rot.free_tok[0]]
        for j in range(NT):
            sp = j % 2
            last_ev = None
            for g in range(4):
                bk = (j * 4 + g) % 2
                tt = None
                for kc4 in range(4):
                    kc = g * 4 + kc4
                    w = ([tb_free[bk]] if tb_free[bk] is not None else []) if kc4 == 0 else []
                    if j == 0 and g == 0 and kc4 == 0:
                        w = w + [t_final]
                    tt = S.op("tensor",
                              lambda E, j=j, kc=kc, kc4=kc4, bk=bk:
                              E.transpose(tb[bk][:, kc4 * 128:(kc4 + 1) * 128], x_fm[:, kc, j * 128:(j + 1) * 128], idf_sb[:]),
                              waits=w, inc=("p" if kc4 == 3 else None))
                wx = [xout_free[sp]] if (g == 0 and xout_free[sp] is not None) else []
                last_ev = S.op("vector",
                               lambda E, sp=sp, g=g, bk=bk: E.tensor_copy(xout[sp][:, g * 512:(g + 1) * 512], tb[bk][:]),
                               waits=[tt] + wx, inc="v")
                tb_free[bk] = last_ev
            t_st = S.op("sync",
                        lambda E, j=j, sp=sp: E.dma_start(out_d[j * 128:(j + 1) * 128, :], xout[sp][:]),
                        waits=[last_ev], inc=f"o{sp}", inc_k=16)
            xout_free[sp] = t_st
        S.op("sync", lambda E: E.nop(), waits=[("o0", S.cnt["o0"]), ("o1", S.cnt["o1"])])
        S.flush()
        outS.close()
    return nc


def _host_prep(inputs):
    x = np.ascontiguousarray(np.asarray(inputs["L0_attn_x"], dtype=np.float32).reshape(8192, D))
    latq = np.asarray(inputs["L0_attn_latent_queries"], dtype=np.float32)[0]
    Wk = np.asarray(inputs["L0_attn_Wk"], dtype=np.float32)
    Wv = np.asarray(inputs["L0_attn_Wv"], dtype=np.float32)
    Wg = np.asarray(inputs["L0_attn_Wg"], dtype=np.float32)
    Wp = np.asarray(inputs["L0_attn_Wp"], dtype=np.float32)
    gw = np.asarray(inputs["L0_ffn_gate_w"], dtype=np.float32)
    upw = np.asarray(inputs["L0_ffn_expert_up_w"], dtype=np.float32)
    dnw = np.asarray(inputs["L0_ffn_expert_down_w"], dtype=np.float32)
    sup = np.asarray(inputs["L0_ffn_shared_up_w"], dtype=np.float32)
    sdn = np.asarray(inputs["L0_ffn_shared_down_w"], dtype=np.float32)

    dperm = np.concatenate([np.arange(0, HD, 2), np.arange(1, HD, 2)])
    colperm = np.concatenate([h * HD + dperm for h in range(NH)])
    Wk_p = np.ascontiguousarray(Wk[:, colperm])
    q_p = np.ascontiguousarray(latq.transpose(2, 1, 0)[dperm])

    inv = 1.0 / (10000.0 ** (np.arange(64) * 2.0 / HD))
    ang = np.arange(128)[None, :] * inv[:, None]
    cosr = np.ascontiguousarray(np.cos(ang).astype(np.float32))
    sinr = np.ascontiguousarray(np.sin(ang).astype(np.float32))
    maskr = np.ascontiguousarray(
        np.where(np.arange(128)[None, :] <= np.arange(NL)[:, None], 0.0, -1e9).astype(np.float32))

    idb = np.eye(128, dtype=ml_dtypes.bfloat16)
    idf = np.eye(128, dtype=np.float32)

    def pairperm(wT, ich):
        cols = []
        for i in range(ich):
            cols.append(wT[:, i * 128:(i + 1) * 128])
            cols.append(wT[:, (ich + i) * 128:(ich + i + 1) * 128])
        return np.ascontiguousarray(np.concatenate(cols, axis=1))

    up_p = np.ascontiguousarray(np.stack([pairperm(upw[e].T, EIC) for e in range(NE)]))
    dn_p = np.ascontiguousarray(np.stack([np.ascontiguousarray(dnw[e].T) for e in range(NE)]))
    sup_p = pairperm(np.ascontiguousarray(sup.T), SIC)
    sdn_p = np.ascontiguousarray(sdn.T)
    gw_t = np.ascontiguousarray(gw.T)

    ltri = np.ascontiguousarray(np.triu(np.ones((128, 128)), 1).astype(ml_dtypes.bfloat16))
    ecrow = np.ascontiguousarray((np.arange(NE) * CAP).astype(np.float32).reshape(1, NE))
    thr = np.ascontiguousarray(np.broadcast_to((np.arange(NE) * CAP + CAP).astype(np.float32), (128, NE)).copy())
    tokidf = np.ascontiguousarray((np.arange(NT)[None, :] * 128 + np.arange(128)[:, None]).astype(np.float32))
    bigf = np.full((128, 384), BIGF, dtype=np.float32)

    shared = {
        "xp": np.ascontiguousarray(x[0:128]),
        "ltri": ltri, "ecrow": ecrow, "thr": thr, "tokidf": tokidf, "bigf": bigf,
        "wk": Wk_p, "wv": Wv, "wg": Wg, "wp": Wp,
        "q": q_p, "cosr": cosr, "sinr": sinr, "maskr": maskr,
        "idb": idb, "idf": idf, "gwt": gw_t,
        "upw": up_p, "dnw": dn_p, "supw": sup_p, "sdnw": sdn_p,
    }
    in_maps = []
    for c in range(NCORE):
        m = dict(shared)
        m["x"] = np.ascontiguousarray(x[c * TOK:(c + 1) * TOK])
        in_maps.append(m)
    return in_maps


def kernel(**inputs):
    in_maps = _host_prep(inputs)
    nc = build()
    res = run_bass_kernel_spmd(nc, in_maps, core_ids=list(range(NCORE)))
    out = np.concatenate([np.asarray(r["out"]) for r in res.results], axis=0)
    return out.reshape(1, 8192, D).astype(np.float32)


# version 12
# speedup vs baseline: 1.0545x; 1.0545x over previous
import math
from contextlib import ExitStack

import numpy as np
import ml_dtypes

import concourse.bass as bass
import concourse.mybir as mybir
from concourse.bass_utils import run_bass_kernel_spmd
from concourse.bass_core_rust import IndirectOffsetOnAxis

F32 = mybir.dt.float32
BF16 = mybir.dt.bfloat16
I32 = mybir.dt.int32
AF = mybir.ActivationFunctionType
ALU = mybir.AluOpType
AX = mybir.AxisListType

NCORE = 8
TOK = 1024
NT = TOK // 128
KC = 16
D = 2048
NH = 16
NL = 64
HD = 128
NE = 16
EI = 1408
EIC = EI // 128
SI = 2816
SIC = SI // 128
SCALE = 1.0 / math.sqrt(HD)
EPS = 1e-5
CAP = 384
BIGF = float(1 << 28)

ENGINES = ("sync", "gpsimd", "tensor", "vector", "scalar")


class Sch:

    def __init__(self, nc, sems):
        self.nc = nc
        self.sems = sems
        self.cnt = {k: 0 for k in sems}
        self.ops = None

    def phase(self):
        self.ops = {e: [] for e in ENGINES}

    def flush(self):
        with self.nc.Block() as block:
            for ename in ENGINES:
                fns = self.ops[ename]
                if not fns:
                    continue
                getattr(block, ename)(lambda E, fns=fns: [f(E) for f in fns])
        self.ops = None

    def tok(self, name, k=1):
        self.cnt[name] += k
        return (name, self.cnt[name])

    def op(self, eng, emit, waits=(), inc=None, inc_k=1):
        t = self.tok(inc, inc_k) if inc else None
        sems = self.sems
        waits = tuple(w for w in waits if w is not None)

        def f(E, waits=waits, emit=emit, inc=inc, inc_k=inc_k):
            for (s, v) in waits:
                E.wait_ge(sems[s], v)
            inst = emit(E)
            if inc is not None:
                inst.then_inc(sems[inc], inc_k)
        self.ops[eng].append(f)
        return t

    def mm_group(self, mms, waits=(), inc="p"):
        t = self.tok(inc, 1)
        sems = self.sems
        waits = tuple(w for w in waits if w is not None)

        def f(E, waits=waits, mms=tuple(mms)):
            for (s, v) in waits:
                E.wait_ge(sems[s], v)
            n = len(mms)
            for i, (o, l, r) in enumerate(mms):
                inst = E.matmul(o, l, r, start=(i == 0), stop=(i == n - 1))
            inst.then_inc(sems[inc], 1)
        self.ops["tensor"].append(f)
        return t


class Banks:
    def __init__(self, tensors):
        self.t = tensors
        self.free_tok = [None] * len(tensors)
        self.i = 0

    def next(self):
        b = self.i
        self.i = (self.i + 1) % len(self.t)
        return b

    def waits(self, b):
        return [self.free_tok[b]] if self.free_tok[b] is not None else []

    def set_free(self, b, tokv):
        self.free_tok[b] = tokv


def build(n_experts=NE, debug=False, sparse=True):
    nc = bass.Bass("TRN2", target_bir_lowering=False)

    x_d = nc.dram_tensor("x", [TOK, D], F32, kind="ExternalInput")
    xp_d = nc.dram_tensor("xp", [128, D], F32, kind="ExternalInput")
    wk_d = nc.dram_tensor("wk", [D, D], F32, kind="ExternalInput")
    wv_d = nc.dram_tensor("wv", [D, D], F32, kind="ExternalInput")
    wg_d = nc.dram_tensor("wg", [D, NH * NL], F32, kind="ExternalInput")
    wp_d = nc.dram_tensor("wp", [D, D], F32, kind="ExternalInput")
    q_d = nc.dram_tensor("q", [128, NH, NL], F32, kind="ExternalInput")
    cos_d = nc.dram_tensor("cosr", [64, 128], F32, kind="ExternalInput")
    sin_d = nc.dram_tensor("sinr", [64, 128], F32, kind="ExternalInput")
    mask_d = nc.dram_tensor("maskr", [64, 128], F32, kind="ExternalInput")
    idb_d = nc.dram_tensor("idb", [128, 128], BF16, kind="ExternalInput")
    idf_d = nc.dram_tensor("idf", [128, 128], F32, kind="ExternalInput")
    gw_d = nc.dram_tensor("gwt", [D, NE], F32, kind="ExternalInput")
    up_d = nc.dram_tensor("upw", [NE, D, 2 * EI], F32, kind="ExternalInput")
    dn_d = nc.dram_tensor("dnw", [NE, EI, D], F32, kind="ExternalInput")
    sup_d = nc.dram_tensor("supw", [D, 2 * SI], F32, kind="ExternalInput")
    sdn_d = nc.dram_tensor("sdnw", [SI, D], F32, kind="ExternalInput")
    out_d = nc.dram_tensor("out", [TOK, D], F32, kind="ExternalOutput")
    if sparse:
        ltri_d = nc.dram_tensor("ltri", [128, 128], BF16, kind="ExternalInput")
        thr_d = nc.dram_tensor("thr", [128, NE], F32, kind="ExternalInput")
        ecrow_d = nc.dram_tensor("ecrow", [1, NE], F32, kind="ExternalInput")
        tokidf_d = nc.dram_tensor("tokidf", [128, NT], F32, kind="ExternalInput")
        bigf_d = nc.dram_tensor("bigf", [128, 384], F32, kind="ExternalInput")
        disp_d = nc.dram_tensor("disp", [NE * CAP, D], BF16, kind="Internal")
        aux_d = nc.dram_tensor("aux", [NE * CAP, 8], F32, kind="Internal")
        acc_d = [nc.dram_tensor(f"acc{k}", [TOK, D], BF16, kind="Internal") for k in range(4)]
    if debug:
        dbg_xmid = nc.dram_tensor("dbg_xmid", [D, TOK], F32, kind="ExternalOutput")
        dbg_attn = nc.dram_tensor("dbg_attn", [64, NH * HD], BF16, kind="ExternalOutput")
        dbg_cw = nc.dram_tensor("dbg_cw", [NE, TOK], F32, kind="ExternalOutput")
        dbg_ln2 = nc.dram_tensor("dbg_ln2", [D, TOK], BF16, kind="ExternalOutput")

    S0 = ExitStack()
    with S0:
        sems = {}
        for name in ("q0", "q1", "q2", "q3", "i", "x0", "x1", "o0", "o1", "g2", "g3", "g4", "c0", "c1", "z", "p", "v", "a"):
            sems[name] = S0.enter_context(nc.semaphore("s" + name))
        S = Sch(nc, sems)

        x_fm = S0.enter_context(nc.sbuf_tensor("x_fm", [128, KC, TOK], F32))
        ws = [S0.enter_context(nc.sbuf_tensor(f"ws{i}", [128, KC, 512], BF16)) for i in range(2)]
        attn_sb = S0.enter_context(nc.sbuf_tensor("attn_sb", [128, NH, HD], BF16))
        scr = [S0.enter_context(nc.sbuf_tensor(f"scr{i}", [128, 512], F32)) for i in range(2)]
        onesf = S0.enter_context(nc.sbuf_tensor("onesf", [128, 1], F32))
        epsc = S0.enter_context(nc.sbuf_tensor("epsc", [128, 1], F32))
        ones1 = S0.enter_context(nc.sbuf_tensor("ones1", [1, 128], F32))
        idb_sb = S0.enter_context(nc.sbuf_tensor("idb_sb", [128, 128], BF16))
        idf_sb = S0.enter_context(nc.sbuf_tensor("idf_sb", [128, 128], F32))
        if not sparse:
            cwT = S0.enter_context(nc.sbuf_tensor("cwT", [NE, TOK], F32))
        else:
            xg0 = S0.enter_context(nc.sbuf_tensor("xg0", [128, KC, CAP], BF16))
        gsmall = S0.enter_context(nc.sbuf_tensor("gsmall", [128, 12, 16], F32))
        ssmall = S0.enter_context(nc.sbuf_tensor("ssmall", [64, 4, 16], F32))

        rot_t = [S0.enter_context(nc.psum_tensor(f"rot{i}", [128, 512], F32)) for i in range(6)]
        PT6 = S0.enter_context(nc.psum_tensor("pt6", [128, 512], F32))
        PT7 = S0.enter_context(nc.psum_tensor("pt7", [128, 1024], BF16))
        rot = Banks(rot_t)
        pt6_free = [None]
        pt7_free = [None]
        scr_last_pe = [None]
        scr_last_dve = [None]

        ws_free = [None, None, None, None]
        ws_next = [0]
        ws_mod = [2]

        def stream_w(dram_ap, nchunks, ncols):
            p = ws_next[0]
            ws_next[0] = (ws_next[0] + 1) % ws_mod[0]
            t = S.op("gpsimd",
                     lambda E, d=dram_ap, p=p, nk=nchunks, ncol=ncols:
                     E.dma_start(ws[p][:, 0:nk, 0:ncol], d),
                     waits=[ws_free[p]], inc=f"q{p}", inc_k=16)
            return p, t

        def consume_w(p, tokv):
            ws_free[p] = tokv

        def stream_w2(dram_ap, nchunks, ncols):
            p = ws_next[0]
            ws_next[0] = (ws_next[0] + 1) % ws_mod[0]
            t = S.op("gpsimd",
                     lambda E, d=dram_ap, p=p, nk=nchunks, ncol=ncols:
                     E.dma_start(ws[p][:].rearrange("p a b -> p (a b)").rearrange("p (k m) -> p k m", m=ncol)[:, 0:nk, :], d),
                     waits=[ws_free[p]], inc=f"q{p}", inc_k=16)
            return p, t

        S.phase()
        stageS = ExitStack()
        stage = stageS.enter_context(nc.sbuf_tensor("stage", [128, 2, D], F32))

        tc_id = S.op("sync", lambda E: E.dma_start(idb_sb[:], idb_d[:, :]), inc="i", inc_k=16)
        tc_idf = S.op("sync", lambda E: E.dma_start(idf_sb[:], idf_d[:, :]), inc="i", inc_k=16)
        t_ones = S.op("vector", lambda E: E.memset(onesf[:], 1.0), inc="v")
        t_eps = S.op("vector", lambda E: E.memset(epsc[:], EPS), inc="v")
        t_ones1 = S.op("vector", lambda E: E.memset(ones1[:], 1.0), inc="v")

        wk_cb = []
        for cb in range(2):
            wk_cb.append(stream_w(wk_d[:, cb * 512:(cb + 1) * 512].rearrange("(kc p) n -> p kc n", p=128), KC, 512))

        stage_free = [None, None]
        for j in range(NT):
            sp = j % 2
            t_ld = S.op("sync", lambda E, j=j, sp=sp: E.dma_start(stage[:, sp, :], x_d[j * 128:(j + 1) * 128, :]),
                        waits=[stage_free[sp]], inc=f"x{sp}", inc_k=16)
            for g in range(4):
                tt = None
                for kc4 in range(4):
                    kc = g * 4 + kc4
                    w = ([t_ld, tc_idf] + ([pt6_free[0]] if pt6_free[0] is not None else [])) if kc4 == 0 else []
                    tt = S.op("tensor",
                              lambda E, sp=sp, kc=kc, kc4=kc4:
                              E.transpose(PT6[:, kc4 * 128:(kc4 + 1) * 128],
                                          stage[:, sp, kc * 128:(kc + 1) * 128], idf_sb[:]),
                              waits=w, inc=("p" if kc4 == 3 else None))
                t_ev = S.op("vector",
                            lambda E, j=j, g=g:
                            E.tensor_copy(x_fm[:, g * 4:(g + 1) * 4, j * 128:(j + 1) * 128],
                                          PT6[:].rearrange("p (c n) -> p c n", c=4)),
                            waits=[tt], inc="v")
                pt6_free[0] = t_ev
                if g == 3:
                    stage_free[sp] = t_ev
        S.flush()
        stageS.close()

        lnfS = ExitStack()
        ln_fm = lnfS.enter_context(nc.sbuf_tensor("ln_fm", [128, KC, TOK], BF16))
        preS = ExitStack()
        xp_sb = preS.enter_context(nc.sbuf_tensor("xp_sb", [128, D], F32))
        ptmp = preS.enter_context(nc.sbuf_tensor("ptmp", [128, D], F32))
        lnp_tm = preS.enter_context(nc.sbuf_tensor("lnp_tm", [128, D], BF16))
        lnp_fm = preS.enter_context(nc.sbuf_tensor("lnp_fm", [128, KC, 128], BF16))
        k_fm = preS.enter_context(nc.sbuf_tensor("k_fm", [128, NH, 128], BF16))
        kr = preS.enter_context(nc.sbuf_tensor("kr", [128, NH, 128], BF16))
        v_sb = preS.enter_context(nc.sbuf_tensor("v_sb", [128, NH, 128], BF16))
        v_tm = preS.enter_context(nc.sbuf_tensor("v_tm", [128, NH, 128], BF16))
        wT = preS.enter_context(nc.sbuf_tensor("wT", [128, NH, 64], BF16))
        q_sb = preS.enter_context(nc.sbuf_tensor("q_sb", [128, NH, NL], BF16))
        cos_sb = preS.enter_context(nc.sbuf_tensor("cos_sb", [64, 128], F32))
        sin_sb = preS.enter_context(nc.sbuf_tensor("sin_sb", [64, 128], F32))
        mask_sb = preS.enter_context(nc.sbuf_tensor("mask_sb", [64, 128], F32))
        pcol = preS.enter_context(nc.sbuf_tensor("pcol", [128, 8], F32))

        S.phase()
        tc_cos = S.op("sync", lambda E: E.dma_start(cos_sb[:], cos_d[:, :]), inc="i", inc_k=16)
        tc_sin = S.op("sync", lambda E: E.dma_start(sin_sb[:], sin_d[:, :]), inc="i", inc_k=16)
        tc_mask = S.op("sync", lambda E: E.dma_start(mask_sb[:], mask_d[:, :]), inc="i", inc_k=16)
        tc_q = S.op("gpsimd", lambda E: E.dma_start(q_sb[:], q_d[:, :, :]), inc="g2", inc_k=16)
        tc_xp = S.op("sync", lambda E: E.dma_start(xp_sb[:], xp_d[:, :]), inc="i", inc_k=16)
        tc_all = ("i", S.cnt["i"])

        _ln_n = [0]

        def emit_ln(dst_ln):
            last = None
            _ln_n[0] += 1
            lnS = ExitStack()
            stats = lnS.enter_context(nc.sbuf_tensor(f"stats{_ln_n[0]}", [1, 4, 512], F32))
            ab = lnS.enter_context(nc.sbuf_tensor(f"ab{_ln_n[0]}", [128, 2, 512], F32))
            for half in range(2):
                hsl = slice(half * 512, (half + 1) * 512)
                base_p = S.cnt["p"]
                base_q = base_p + 1
                sqt = [scr[0][:], scr[1][:], ab[:, 0, :], ab[:, 1, :]]
                sq_toks = []
                for kc in range(KC):
                    sp = kc % 4
                    if kc < 4:
                        w = [scr_last_pe[0]]
                    else:
                        w = [("p", base_q + (kc - 4) // 4 + 1)]
                    t_sq = S.op("vector",
                                lambda E, kc=kc, sp=sp, hsl=hsl:
                                E.tensor_mul(sqt[sp], x_fm[:, kc, hsl], x_fm[:, kc, hsl]),
                                waits=w, inc="v")
                    sq_toks.append((sp, t_sq))
                b_s, b_q = 4, 5
                mms = [(rot_t[b_s][0:1, :], onesf[:], x_fm[:, kc, hsl]) for kc in range(KC)]
                t_gs = S.mm_group(mms, waits=rot.waits(b_s) + [t_ones])
                assert t_gs[1] == base_p + 1
                for (s, v) in rot.waits(b_q):
                    S.ops["tensor"].append(lambda E, s=s, v=v: E.wait_ge(sems[s], v))
                t_gq = None
                for kc in range(KC):
                    sp, t_sq = sq_toks[kc]
                    if kc % 4 == 3:
                        t_gq = S.tok("p")

                    def femit(E, sp=sp, t_sq=t_sq, first=(kc == 0), last_=(kc == KC - 1), do_inc=(kc % 4 == 3)):
                        E.wait_ge(sems["v"], t_sq[1])
                        inst = E.matmul(rot_t[b_q][0:1, :], onesf[:], sqt[sp],
                                        start=first, stop=last_)
                        if do_inc:
                            inst.then_inc(sems["p"], 1)
                    S.ops["tensor"].append(femit)
                scr_last_pe[0] = t_gq
                t1 = S.op("vector", lambda E: E.tensor_scalar(stats[0:1, 0, :], rot_t[b_s][0:1, :], 1.0 / D, None, ALU.mult),
                          waits=[t_gs], inc="v")
                t2 = S.op("vector", lambda E: E.tensor_scalar(stats[0:1, 1, :], rot_t[b_q][0:1, :], 1.0 / D, None, ALU.mult),
                          waits=[t_gq], inc="v")
                t3 = S.op("vector", lambda E: E.tensor_mul(stats[0:1, 2, :], stats[0:1, 0, :], stats[0:1, 0, :]),
                          waits=[t1], inc="v")
                t4 = S.op("vector", lambda E: E.tensor_sub(stats[0:1, 1, :], stats[0:1, 1, :], stats[0:1, 2, :]),
                          waits=[t2, t3], inc="v")
                t5 = S.op("scalar", lambda E: E.activation(stats[0:1, 3, :], stats[0:1, 1, :], AF.Ln, bias=epsc[0:1, 0:1], scale=1.0),
                          waits=[t4, t_eps], inc="a")
                t6 = S.op("scalar", lambda E: E.activation(stats[0:1, 2, :], stats[0:1, 3, :], AF.Exp, bias=0.0, scale=-0.5),
                          waits=[t5], inc="a")
                t7 = S.op("vector", lambda E: E.scalar_tensor_tensor(stats[0:1, 3, :], stats[0:1, 0, :], -1.0, stats[0:1, 2, :], ALU.mult, ALU.mult),
                          waits=[t6], inc="v")
                b_a, b_b = 2, 3
                ta = S.mm_group([(rot_t[b_a][:, :], ones1[:], stats[0:1, 2, :])],
                                waits=rot.waits(b_a) + [t6, t_ones1])
                tb = S.mm_group([(rot_t[b_b][:, :], ones1[:], stats[0:1, 3, :])],
                                waits=rot.waits(b_b) + [t7])
                tae = S.op("vector", lambda E: E.tensor_copy(ab[:, 0, :], rot_t[b_a][:]), waits=[ta], inc="v")
                tbe = S.op("vector", lambda E: E.tensor_copy(ab[:, 1, :], rot_t[b_b][:]), waits=[tb], inc="v")
                rot.set_free(b_a, tae)
                rot.set_free(b_b, tbe)
                rot.set_free(b_s, t1)
                rot.set_free(b_q, t2)
                for kc in range(KC):
                    sp = kc % 2
                    S.op("vector",
                         lambda E, kc=kc, sp=sp, hsl=hsl:
                         E.tensor_mul(scr[sp][:], x_fm[:, kc, hsl], ab[:, 0, :]),
                         waits=([tae, tbe, scr_last_pe[0]] if kc < 2 else []), inc="v")
                    last = S.op("vector",
                                lambda E, kc=kc, sp=sp, hsl=hsl, dst=dst_ln:
                                E.tensor_add(dst[:, kc, hsl], scr[sp][:], ab[:, 1, :]),
                                inc="v")
            lnS.close()
            return last

        t_ln1 = emit_ln(ln_fm)

        tp1 = S.op("vector", lambda E: E.reduce_sum(pcol[:, 0:1], xp_sb[:], axis=AX.X), waits=[tc_all], inc="v")
        tp2 = S.op("vector", lambda E: E.tensor_scalar(pcol[:, 1:2], pcol[:, 0:1], 1.0 / D, None, ALU.mult), waits=[tp1], inc="v")
        tp3 = S.op("vector", lambda E: E.tensor_mul(ptmp[:], xp_sb[:], xp_sb[:]), waits=[tc_all], inc="v")
        tp4 = S.op("vector", lambda E: E.reduce_sum(pcol[:, 2:3], ptmp[:], axis=AX.X), waits=[tp3], inc="v")
        tp5 = S.op("vector", lambda E: E.tensor_scalar(pcol[:, 3:4], pcol[:, 2:3], 1.0 / D, None, ALU.mult), waits=[tp4], inc="v")
        tp6 = S.op("vector", lambda E: E.tensor_mul(pcol[:, 4:5], pcol[:, 1:2], pcol[:, 1:2]), waits=[tp2], inc="v")
        tp7 = S.op("vector", lambda E: E.tensor_sub(pcol[:, 5:6], pcol[:, 3:4], pcol[:, 4:5]), waits=[tp5, tp6], inc="v")
        tp8 = S.op("scalar", lambda E: E.activation(pcol[:, 6:7], pcol[:, 5:6], AF.Ln, bias=epsc[:, 0:1], scale=1.0), waits=[tp7, t_eps], inc="a")
        tp9 = S.op("scalar", lambda E: E.activation(pcol[:, 7:8], pcol[:, 6:7], AF.Exp, bias=0.0, scale=-0.5), waits=[tp8], inc="a")
        tp10 = S.op("vector", lambda E: E.scalar_tensor_tensor(pcol[:, 4:5], pcol[:, 1:2], -1.0, pcol[:, 7:8], ALU.mult, ALU.mult), waits=[tp9], inc="v")
        tp11 = S.op("vector", lambda E: E.tensor_scalar(lnp_tm[:], xp_sb[:], pcol[:, 7:8], pcol[:, 4:5], ALU.mult, ALU.add), waits=[tp10], inc="v")
        for g in range(2):
            tt = None
            for k8 in range(8):
                kc = g * 8 + k8
                w = ([tp11, tc_id] + ([pt7_free[0]] if pt7_free[0] is not None else [])) if k8 == 0 else []
                tt = S.op("tensor",
                          lambda E, kc=kc, k8=k8:
                          E.transpose(PT7[:, k8 * 128:(k8 + 1) * 128], lnp_tm[:, kc * 128:(kc + 1) * 128], idb_sb[:]),
                          waits=w, inc=("p" if k8 == 7 else None))
            t_ev = S.op("vector",
                        lambda E, g=g:
                        E.tensor_copy(lnp_fm[:, g * 8:(g + 1) * 8, :], PT7[:].rearrange("p (c n) -> p c n", c=8)),
                        waits=[tt], inc="v")
            pt7_free[0] = t_ev
        t_lnp = pt7_free[0]
        t_q = S.op("vector", lambda E: E.tensor_scalar(q_sb[:].rearrange("d h l -> d (h l)"), q_sb[:].rearrange("d h l -> d (h l)"), SCALE, None, ALU.mult),
                   waits=[tc_q], inc="v")
        S.flush()

        rp3S = ExitStack()
        rp3 = rp3S.enter_context(nc.sbuf_tensor("rp3", [64, D], F32))
        S.phase()
        kv_evac = {}
        for cb in range(4):
            if cb >= 2:
                wk_cb.append(stream_w(wk_d[:, cb * 512:(cb + 1) * 512].rearrange("(kc p) n -> p kc n", p=128), KC, 512))
            p, t_rdy = wk_cb[cb]
            b = rot.next()
            tg = None
            for oc4 in range(4):
                mms = [(rot_t[b][:, oc4 * 128:(oc4 + 1) * 128],
                        ws[p][:, kc, oc4 * 128:(oc4 + 1) * 128],
                        lnp_fm[:, kc, :]) for kc in range(KC)]
                w = ([t_rdy, t_lnp] + rot.waits(b)) if oc4 == 0 else []
                tg = S.mm_group(mms, waits=w)
            consume_w(p, tg)
            t_ev = S.op("vector",
                        lambda E, cb=cb, b=b:
                        E.tensor_copy(k_fm[:, cb * 4:(cb + 1) * 4, :], rot_t[b][:].rearrange("p (c n) -> p c n", c=4)),
                        waits=[tg], inc="v")
            rot.set_free(b, t_ev)
            kv_evac[("k", cb)] = t_ev
        for cb in range(4):
            p, t_rdy = stream_w(wv_d[:, cb * 512:(cb + 1) * 512].rearrange("(kc p) n -> p kc n", p=128), KC, 512)
            b = rot.next()
            tg = None
            for oc4 in range(4):
                mms = [(rot_t[b][:, oc4 * 128:(oc4 + 1) * 128],
                        ws[p][:, kc, oc4 * 128:(oc4 + 1) * 128],
                        lnp_fm[:, kc, :]) for kc in range(KC)]
                w = ([t_rdy] + rot.waits(b)) if oc4 == 0 else []
                tg = S.mm_group(mms, waits=w)
            consume_w(p, tg)
            t_ev = S.op("vector",
                        lambda E, cb=cb, b=b:
                        E.tensor_copy(v_sb[:, cb * 4:(cb + 1) * 4, :], rot_t[b][:].rearrange("p (c n) -> p c n", c=4)),
                        waits=[tg], inc="v")
            rot.set_free(b, t_ev)
            kv_evac[("v", cb)] = t_ev

        kw = [kv_evac[("k", cb)] for cb in range(4)] + [tc_all]
        cosb = bass.AP(cos_sb, 0, [[cos_sb.ap().ap[0][0], 64], [0, NH], [1, 128]])
        sinb = bass.AP(sin_sb, 0, [[sin_sb.ap().ap[0][0], 64], [0, NH], [1, 128]])

        def phv(t):
            return t.rearrange("p (h n) -> p h n", h=NH)
        R1 = phv(ptmp[0:64, :])
        R2 = phv(xp_sb[0:64, :])
        R3 = phv(rp3[0:64, :])
        KR1 = phv(lnp_tm[0:64, :])
        t_x1 = S.op("gpsimd", lambda E: E.dma_start(R2, k_fm[64:128, :, :]), waits=kw, inc="g2", inc_k=16)
        tr1 = S.op("vector", lambda E: E.tensor_mul(R1, k_fm[0:64, :, :], cosb), waits=kw, inc="v")
        tr2 = S.op("vector", lambda E: E.tensor_mul(R3, R2, sinb), waits=[t_x1], inc="v")
        tr3 = S.op("vector", lambda E: E.tensor_sub(kr[0:64, :, :], R1, R3), waits=[tr1, tr2], inc="v")
        tr4 = S.op("vector", lambda E: E.tensor_mul(R1, k_fm[0:64, :, :], sinb), waits=[tr3], inc="v")
        tr5 = S.op("vector", lambda E: E.tensor_mul(R3, R2, cosb), inc="v")
        tr6 = S.op("vector", lambda E: E.tensor_add(KR1, R1, R3), waits=[tr4, tr5], inc="v")
        t_kr1 = S.op("gpsimd", lambda E: E.dma_start(kr[64:128, :, :], KR1), waits=[tr6], inc="g2", inc_k=16)
        t_rope = tr3
        t_rope_dma = t_kr1

        we = ptmp
        smax_t = {}
        for h in range(NH):
            b = rot.next()
            tg = S.mm_group([(rot_t[b][0:64, 0:128], q_sb[:, h, :], kr[:, h, :])],
                            waits=([t_rope, t_rope_dma, t_q] if h == 0 else []) + rot.waits(b))
            t_s2 = S.op("vector",
                        lambda E, h=h, b=b: E.tensor_add(we[0:64, h * 128:(h + 1) * 128], rot_t[b][0:64, 0:128], mask_sb[:]),
                        waits=[tg, tc_all], inc="v")
            rot.set_free(b, t_s2)
            t_mx = S.op("vector",
                        lambda E, h=h: E.reduce_max(ssmall[:, 0, h:h + 1], we[0:64, h * 128:(h + 1) * 128], axis=AX.X, negate=True),
                        waits=[t_s2], inc="v")
            t_ex = S.op("scalar",
                        lambda E, h=h: E.activation(we[0:64, h * 128:(h + 1) * 128], we[0:64, h * 128:(h + 1) * 128], AF.Exp, bias=ssmall[:, 0, h:h + 1], scale=1.0),
                        waits=[t_mx], inc="a")
            t_sm = S.op("vector",
                        lambda E, h=h: E.reduce_sum(ssmall[:, 1, h:h + 1], we[0:64, h * 128:(h + 1) * 128], axis=AX.X),
                        waits=[t_ex], inc="v")
            t_rc = S.op("vector",
                        lambda E, h=h: E.reciprocal(ssmall[:, 2, h:h + 1], ssmall[:, 1, h:h + 1]),
                        waits=[t_sm], inc="v")
            smax_t[h] = (t_ex, t_rc)
        for h in range(NH):
            t_ex, t_rc = smax_t[h]
            tt = S.op("tensor",
                      lambda E, h=h: E.transpose(PT6[:, 0:64], we[0:64, h * 128:(h + 1) * 128], idf_sb[0:64, 0:64]),
                      waits=[t_ex] + ([pt6_free[0]] if pt6_free[0] is not None else []), inc="p")
            t_ev = S.op("vector",
                        lambda E, h=h: E.tensor_copy(wT[:, h, :], PT6[:, 0:64]),
                        waits=[tt], inc="v")
            pt6_free[0] = t_ev
            tt2 = S.op("tensor",
                       lambda E, h=h: E.transpose(PT7[:, 0:128], v_sb[:, h, :], idb_sb[:]),
                       waits=[kv_evac[("v", h // 4)]] + ([pt7_free[0]] if pt7_free[0] is not None else []), inc="p")
            t_ev2 = S.op("vector",
                         lambda E, h=h: E.tensor_copy(v_tm[:, h, :], PT7[:, 0:128]),
                         waits=[tt2], inc="v")
            pt7_free[0] = t_ev2
            b = rot.next()
            tg = S.mm_group([(rot_t[b][0:64, 0:128], wT[:, h, :], v_tm[:, h, :])],
                            waits=[t_ev, t_ev2] + rot.waits(b))
            t_at = S.op("vector",
                        lambda E, h=h, b=b: E.tensor_scalar(attn_sb[0:64, h, :], rot_t[b][0:64, 0:128], ssmall[:, 2, h:h + 1], None, ALU.mult),
                        waits=[tg, t_rc], inc="v")
            t_at2 = S.op("vector",
                         lambda E, h=h, b=b: E.tensor_scalar(attn_sb[64:128, h, :], rot_t[b][0:64, 0:128], ssmall[:, 2, h:h + 1], None, ALU.mult),
                         waits=[], inc="v")
            rot.set_free(b, t_at2)
        t_attn_all = ("v", S.cnt["v"])
        if debug:
            S.op("sync", lambda E: E.dma_start(dbg_attn.rearrange("p (h n) -> p h n", h=NH), attn_sb[0:64, :, :]),
                 waits=[t_attn_all], inc="i", inc_k=16)
        wg_pre = [stream_w(wg_d[:, og * 512:(og + 1) * 512].rearrange("(kc p) n -> p kc n", p=128), KC, 512)
                  for og in range(2)]
        S.flush()
        rp3S.close()
        preS.close()

        S.phase()
        abufS = ExitStack()
        gates_tm = abufS.enter_context(nc.sbuf_tensor("gates_tm", [128, 1024], BF16))
        g_sb = abufS.enter_context(nc.sbuf_tensor("g_sb", [128, 512], F32))
        if not sparse:
            crow = abufS.enter_context(nc.sbuf_tensor("crow", [1, TOK], F32))
            cwb = [abufS.enter_context(nc.sbuf_tensor(f"cwb{i}", [128, TOK], F32)) for i in range(2)]
        abuf0S = ExitStack()
        abuf = abuf0S.enter_context(nc.sbuf_tensor("abuf", [128, 24, 512], BF16))
        for half in range(2):
            hsl = slice(half * 512, (half + 1) * 512)
            wg_cb = wg_pre if half == 0 else [
                stream_w(wg_d[:, og * 512:(og + 1) * 512].rearrange("(kc p) n -> p kc n", p=128), KC, 512)
                for og in range(2)]
            t_gt_pe = None
            wg_last = [None, None]
            for tj in range(4):
                jt = half * 4 + tj
                for og in range(2):
                    p, t_rdy = wg_cb[og]
                    b = rot.next()
                    mms = [(rot_t[b][:, :], ln_fm[:, kc, jt * 128:(jt + 1) * 128], ws[p][:, kc, :])
                           for kc in range(KC)]
                    tg = S.mm_group(mms, waits=[t_rdy, t_ln1] + rot.waits(b))
                    wg_last[og] = tg
                    t_mx = S.op("vector",
                                lambda E, b=b: E.reduce_max(gsmall[:, 0, 0:8], rot_t[b][:].rearrange("p (h l) -> p h l", h=8), axis=AX.X, negate=True),
                                waits=[tg], inc="v")
                    t_sc = S.op("vector",
                                lambda E: E.tensor_scalar(gsmall[:, 1, 0:8], gsmall[:, 0, 0:8], SCALE, None, ALU.mult),
                                waits=[t_mx], inc="v")
                    t_exs = []
                    for h8 in range(8):
                        t_ex = S.op("scalar",
                                    lambda E, h8=h8, b=b: E.activation(g_sb[:, h8 * 64:(h8 + 1) * 64], rot_t[b][:, h8 * 64:(h8 + 1) * 64], AF.Exp, bias=gsmall[:, 1, h8:h8 + 1], scale=SCALE),
                                    waits=[t_sc] if h8 == 0 else [], inc="a")
                        t_exs.append(t_ex)
                    rot.set_free(b, t_exs[-1])
                    t_sm = S.op("vector",
                                lambda E: E.reduce_sum(gsmall[:, 2, 0:8], g_sb[:].rearrange("p (h l) -> p h l", h=8), axis=AX.X),
                                waits=[t_exs[-1]], inc="v")
                    t_rc = S.op("vector",
                                lambda E: E.reciprocal(gsmall[:, 3, 0:8], gsmall[:, 2, 0:8]),
                                waits=[t_sm], inc="v")
                    for h8 in range(8):
                        w = [t_rc] if h8 == 0 else []
                        if h8 == 0 and og == 0 and t_gt_pe is not None:
                            w = w + [t_gt_pe]
                        t_nm = S.op("vector",
                                    lambda E, h8=h8, og=og: E.tensor_scalar(gates_tm[:, og * 512 + h8 * 64:og * 512 + (h8 + 1) * 64], g_sb[:, h8 * 64:(h8 + 1) * 64], gsmall[:, 3, h8:h8 + 1], None, ALU.mult),
                                    waits=w, inc="v")
                for ch in range(8):
                    tt = S.op("tensor",
                              lambda E, ch=ch: E.transpose(PT7[:, 0:128], gates_tm[:, ch * 128:(ch + 1) * 128], idb_sb[:]),
                              waits=[t_nm] + ([pt7_free[0]] if pt7_free[0] is not None else []) if ch == 0 else ([pt7_free[0]] if pt7_free[0] is not None else []),
                              inc="p")
                    t_gt_pe = tt
                    t_ev = S.op("vector",
                                lambda E, ch=ch, tj=tj: E.tensor_copy(abuf[:, ch, tj * 128:(tj + 1) * 128], PT7[:, 0:128]),
                                waits=[tt], inc="v")
                    pt7_free[0] = t_ev
            for og in range(2):
                consume_w(wg_cb[og][0], wg_last[og])
            t_gfm = pt7_free[0]
            t_wtd = None
            for h in range(NH):
                b = rot.next()
                p0 = (h % 2) * 64
                tg = S.mm_group([(rot_t[b][:, :], attn_sb[p0:p0 + 64, h, :],
                                  abuf[p0:p0 + 64, h // 2, :])],
                                waits=[t_gfm, t_attn_all] + rot.waits(b))
                t_wtd = S.op("vector",
                             lambda E, h=h, b=b: E.tensor_copy(abuf[:, 8 + h, :], rot_t[b][:]),
                             waits=[tg], inc="v")
                rot.set_free(b, t_wtd)
            for cb in range(4):
                p, t_rdy = stream_w(wp_d[:, cb * 512:(cb + 1) * 512].rearrange("(kc p) n -> p kc n", p=128), KC, 512)
                tg = None
                for c in range(4):
                    b = rot.next()
                    mms = [(rot_t[b][:, :], ws[p][:, kc, c * 128:(c + 1) * 128], abuf[:, 8 + kc, :])
                           for kc in range(KC)]
                    w = ([t_rdy, t_wtd] if c == 0 else []) + rot.waits(b)
                    tg = S.mm_group(mms, waits=w)
                    t_ev = S.op("vector",
                                lambda E, cb=cb, c=c, b=b, hsl=hsl:
                                E.tensor_add(x_fm[:, cb * 4 + c, hsl], x_fm[:, cb * 4 + c, hsl], rot_t[b][:]),
                                waits=[tg], inc="v")
                    rot.set_free(b, t_ev)
                consume_w(p, tg)
        t_xmid = ("v", S.cnt["v"])
        if debug:
            S.op("sync", lambda E: E.dma_start(dbg_xmid.rearrange("(kc p) t -> p kc t", p=128), x_fm[:]),
                 waits=[t_xmid], inc="i", inc_k=16)
        S.flush()
        if sparse:
            abuf0S.close()

        S.phase()
        t_ln2 = emit_ln(ln_fm)
        if debug:
            S.op("sync", lambda E: E.dma_start(dbg_ln2.rearrange("(kc p) t -> p kc t", p=128), ln_fm[:]),
                 waits=[t_ln2], inc="i", inc_k=16)
        S.flush()

        fS = ExitStack()
        if sparse:
            lntm2 = fS.enter_context(nc.sbuf_tensor("lntm2", [128, 2, D], BF16))
            selb = fS.enter_context(nc.sbuf_tensor("selb", [128, NE], BF16))
            gs2 = fS.enter_context(nc.sbuf_tensor("gs2", [128, 6, NE], F32))
            becr = fS.enter_context(nc.sbuf_tensor("becr", [1, 2, NE], F32))
            skf = fS.enter_context(nc.sbuf_tensor("skf", [128, NT, 4], F32))
            cwf = fS.enter_context(nc.sbuf_tensor("cwf", [128, NT, 4], F32))
            auxsrc = fS.enter_context(nc.sbuf_tensor("auxsrc", [128, NT, 4, 8], F32))
            ski = fS.enter_context(nc.sbuf_tensor("ski", [128, NT, 4], I32))
            ltri_sb = fS.enter_context(nc.sbuf_tensor("ltri_sb", [128, 128], BF16))
            thr_sb = fS.enter_context(nc.sbuf_tensor("thr_sb", [128, NE], F32))
            tokidf_sb = fS.enter_context(nc.sbuf_tensor("tokidf_sb", [128, NT], F32))
            bigf_sb = fS.enter_context(nc.sbuf_tensor("bigf_sb", [128, 384], F32))
            onesb = fS.enter_context(nc.sbuf_tensor("onesb", [128, 1], BF16))
            cwT = fS.enter_context(nc.sbuf_tensor("cwT", [NE, TOK], F32))
            gwbuf = fS.enter_context(nc.sbuf_tensor("gwbuf", [128, KC, NE], BF16))
        S.phase()
        if sparse:
            tf1 = S.op("sync", lambda E: E.dma_start(ltri_sb[:], ltri_d[:, :]), inc="i", inc_k=16)
            tf2 = S.op("sync", lambda E: E.dma_start(thr_sb[:], thr_d[:, :]), inc="i", inc_k=16)
            tf3 = S.op("sync", lambda E: E.dma_start(becr[0:1, 0, :], ecrow_d[0:1, :]), inc="i", inc_k=16)
            tf4 = S.op("sync", lambda E: E.dma_start(tokidf_sb[:], tokidf_d[:, :]), inc="i", inc_k=16)
            tf5 = S.op("sync", lambda E: E.dma_start(bigf_sb[:], bigf_d[:, :]), inc="i", inc_k=16)
            tf_all = ("i", S.cnt["i"])
            t_onesb = S.op("vector", lambda E: E.memset(onesb[:], 1.0), inc="v")
            t_auxi = S.op("vector", lambda E: E.memset(auxsrc[:].rearrange("p a b c -> p (a b c)"), BIGF), inc="v")
            t_big = S.op("gpsimd",
                         lambda E: E.dma_start(aux_d[:, :].rearrange("(p a) o -> p (a o)", p=128), bigf_sb[:]),
                         waits=[tf_all], inc="g4", inc_k=16)
            lntm_free = [None, None]
            becr_tok = [None]
            pending_disp = [None]
            deferred = []
            r_bdisp = nc.gpsimd.alloc_register("r_bdisp")
            r_bacc = nc.gpsimd.alloc_register("r_bacc")
            S.op("gpsimd", lambda E: E.reg_mov(r_bdisp, NE * CAP - 1))
            S.op("gpsimd", lambda E: E.reg_mov(r_bacc, TOK - 1))
        scr0_war = [None]
        if sparse:
            t_gw = S.op("gpsimd", lambda E: E.dma_start(gwbuf[:], gw_d.rearrange("(kc p) n -> p kc n", p=128)),
                        inc="g2", inc_k=16)
            sharedS = ExitStack()
            asab = sharedS.enter_context(nc.sbuf_tensor("asab", [128, SIC, 512], BF16))

        def gen_shared():
            for half in range(2):
                hsl = slice(half * 512, (half + 1) * 512)
                tg_last = None
                for cb in range(11):
                    p, t_rdy = stream_w(sup_d[:, cb * 512:(cb + 1) * 512].rearrange("(kc p) n -> p kc n", p=128), KC, 512)
                    for pr in range(2):
                        bg = rot.next()
                        bu = rot.next()
                        mms_g = [(rot_t[bg][:, :], ws[p][:, kc, pr * 256:pr * 256 + 128], ln_fm[:, kc, hsl]) for kc in range(KC)]
                        mms_u = [(rot_t[bu][:, :], ws[p][:, kc, pr * 256 + 128:pr * 256 + 256], ln_fm[:, kc, hsl]) for kc in range(KC)]
                        wg_ = [t_rdy] if pr == 0 else []
                        ic = cb * 2 + pr
                        tgg = S.mm_group(mms_g, waits=wg_ + rot.waits(bg))
                        tgu = S.mm_group(mms_u, waits=rot.waits(bu))
                        tg_last = tgu
                        t_sg = S.op("scalar",
                                    lambda E, bg=bg: E.activation(scr[0][:], rot_t[bg][:], AF.Silu),
                                    waits=[tgg] + ([scr0_war[0]] if scr0_war[0] is not None else []), inc="a")
                        t_a = S.op("vector",
                                   lambda E, ic=ic, bu=bu: E.tensor_mul(asab[:, ic, :], scr[0][:], rot_t[bu][:]),
                                   waits=[t_sg, tgu], inc="v")
                        scr0_war[0] = t_a
                        rot.set_free(bg, t_a)
                        rot.set_free(bu, t_a)
                    consume_w(p, tg_last)
                    yield
                t_as = ("v", S.cnt["v"])
                tg = None
                for cb in range(8):
                    p, t_rdy = stream_w2(sdn_d[:, cb * 256:(cb + 1) * 256].rearrange("(kc p) n -> p kc n", p=128), SIC, 256)
                    wv22 = ws[p][:].rearrange("p a b -> p (a b)").rearrange("p (k m) -> p k m", m=256)
                    for c in range(2):
                        b = rot.next()
                        mms = [(rot_t[b][:, :], wv22[:, kc, c * 128:(c + 1) * 128], asab[:, kc, :])
                               for kc in range(SIC)]
                        w = ([t_rdy, t_as] if (c == 0 and cb == 0) else ([t_rdy] if c == 0 else [])) + rot.waits(b)
                        tg = S.mm_group(mms, waits=w)
                        t_ev = S.op("vector",
                                    lambda E, cb=cb, c=c, b=b, hsl=hsl:
                                    E.tensor_add(x_fm[:, cb * 2 + c, hsl], x_fm[:, cb * 2 + c, hsl], rot_t[b][:]),
                                    waits=[tg], inc="v")
                        rot.set_free(b, t_ev)
                    consume_w(p, tg)
                    yield

        if not sparse:
            p_gw, t_gw = stream_w(gw_d.rearrange("(kc p) n -> p kc n", p=128), KC, NE)
        fout = {}

        def gen_tj():
          tg = None
          for tj in range(8):
            b = rot.next()
            gwsrc = gwbuf if sparse else ws[p_gw]
            mms = [(rot_t[b][:, 0:NE], ln_fm[:, kc, tj * 128:(tj + 1) * 128], gwsrc[:, kc, 0:NE])
                   for kc in range(KC)]
            tg = S.mm_group(mms, waits=[t_gw, t_ln2] + rot.waits(b))
            t1 = S.op("vector", lambda E, b=b: E.reduce_max(gsmall[:, 0, 0:1], rot_t[b][:, 0:NE], axis=AX.X, negate=True), waits=[tg], inc="v")
            t2 = S.op("scalar", lambda E, b=b: E.activation(gsmall[:, 3, 0:NE], rot_t[b][:, 0:NE], AF.Exp, bias=gsmall[:, 0, 0:1], scale=1.0), waits=[t1], inc="a")
            rot.set_free(b, t2)
            t6 = S.op("vector", lambda E: E.reduce_max(gsmall[:, 4, 0:4], gsmall[:, 3, 0:NE].rearrange("p (g e) -> p g e", g=4), axis=AX.X), waits=[t2], inc="v")
            tm1 = S.op("vector", lambda E: E.reduce_max(gsmall[:, 5, 0:1], gsmall[:, 4, 0:4], axis=AX.X), waits=[t6], inc="v")
            tm2 = S.op("vector", lambda E: E.tensor_scalar(gsmall[:, 5, 8:12], gsmall[:, 4, 0:4], gsmall[:, 5, 0:1], None, ALU.is_lt), waits=[tm1], inc="v")
            tm3 = S.op("vector", lambda E: E.tensor_mul(gsmall[:, 6, 0:4], gsmall[:, 4, 0:4], gsmall[:, 5, 8:12]), waits=[tm2], inc="v")
            tm4 = S.op("vector", lambda E: E.reduce_max(gsmall[:, 5, 1:2], gsmall[:, 6, 0:4], axis=AX.X), waits=[tm3], inc="v")
            t9 = S.op("vector", lambda E: E.tensor_scalar(gsmall[:, 6, 0:4], gsmall[:, 4, 0:4], gsmall[:, 5, 1:2], None, ALU.is_ge), waits=[tm4], inc="v")
            tA = None
            for gp in range(4):
                tA = S.op("vector", lambda E, gp=gp: E.tensor_scalar(gsmall[:, 7, gp * 4:(gp + 1) * 4], gsmall[:, 3, gp * 4:(gp + 1) * 4], gsmall[:, 6, gp:gp + 1], None, ALU.mult), waits=[t9] if gp == 0 else [], inc="v")
            tw = S.op("vector", lambda E: E.tensor_copy(gsmall[:, 8, 0:NE], gsmall[:, 7, 0:NE]), waits=[tA], inc="v")
            for step in range(3):
                tx = S.op("vector", lambda E, step=step: E.reduce_max(gsmall[:, 5, 2 + step:3 + step], gsmall[:, 8, 0:NE], axis=AX.X), waits=[tw], inc="v")
                ty = S.op("vector", lambda E, step=step: E.tensor_scalar(gsmall[:, 9, 0:NE], gsmall[:, 8, 0:NE], gsmall[:, 5, 2 + step:3 + step], None, ALU.is_lt), waits=[tx], inc="v")
                tw = S.op("vector", lambda E: E.tensor_mul(gsmall[:, 8, 0:NE], gsmall[:, 8, 0:NE], gsmall[:, 9, 0:NE]), waits=[ty], inc="v")
            tth = S.op("vector", lambda E: E.reduce_max(gsmall[:, 5, 5:6], gsmall[:, 8, 0:NE], axis=AX.X), waits=[tw], inc="v")
            tD = S.op("vector", lambda E: E.tensor_scalar(gsmall[:, 9, 0:NE], gsmall[:, 7, 0:NE], gsmall[:, 5, 5:6], None, ALU.is_ge), waits=[tth], inc="v")
            tE = S.op("vector", lambda E: E.tensor_mul(gsmall[:, 10, 0:NE], gsmall[:, 7, 0:NE], gsmall[:, 9, 0:NE]), waits=[tD], inc="v")
            tF = S.op("vector", lambda E: E.reduce_sum(gsmall[:, 11, 0:1], gsmall[:, 10, 0:NE], axis=AX.X), waits=[tE], inc="v")
            tG = S.op("vector", lambda E: E.tensor_scalar(gsmall[:, 11, 1:2], gsmall[:, 11, 0:1], 1e-20, None, ALU.add), waits=[tF], inc="v")
            tH = S.op("vector", lambda E: E.reciprocal(gsmall[:, 11, 2:3], gsmall[:, 11, 1:2]), waits=[tG], inc="v")
            tI = S.op("vector", lambda E: E.tensor_scalar(gsmall[:, 10, 0:NE], gsmall[:, 10, 0:NE], gsmall[:, 11, 2:3], None, ALU.mult), waits=[tH], inc="v")
            if sparse:
                yield
            tt = S.op("tensor", lambda E: E.transpose(PT6[0:NE, 0:128], gsmall[:, 10, 0:NE], idf_sb[:]),
                      waits=[tI] + ([pt6_free[0]] if pt6_free[0] is not None else []), inc="p")
            t_ev = S.op("vector", lambda E, tj=tj: E.tensor_copy(cwT[:, tj * 128:(tj + 1) * 128], PT6[0:NE, 0:128]), waits=[tt], inc="v")
            pt6_free[0] = t_ev
            if not sparse:
                continue
            t_selb = S.op("vector", lambda E: E.tensor_copy(selb[:], gsmall[:, 9, 0:NE]), waits=[tD], inc="v")
            bR = rot.next()
            tgR = S.mm_group([(rot_t[bR][:, 0:NE], ltri_sb[:], selb[:])],
                             waits=[t_selb, tf_all] + rot.waits(bR))
            bC = rot.next()
            tgC = S.mm_group([(rot_t[bC][0:1, 0:NE], onesb[:], selb[:])],
                             waits=[t_onesb] + rot.waits(bC))
            bB = rot.next()
            tgB = S.mm_group([(rot_t[bB][:, 0:NE], ones1[:], becr[0:1, 0, :])],
                             waits=([tf_all] if tj == 0 else [becr_tok[0]]) + rot.waits(bB))
            t_r = S.op("vector", lambda E, bR=bR: E.tensor_copy(gs2[:, 0, :], rot_t[bR][:, 0:NE]), waits=[tgR], inc="v")
            rot.set_free(bR, t_r)
            t_slot = S.op("vector", lambda E, bB=bB: E.tensor_add(gs2[:, 1, :], gs2[:, 0, :], rot_t[bB][:, 0:NE]), waits=[t_r, tgB], inc="v")
            rot.set_free(bB, t_slot)
            t_cnt = S.op("vector", lambda E, bC=bC: E.tensor_copy(becr[0:1, 1, :], rot_t[bC][0:1, 0:NE]), waits=[tgC], inc="v")
            rot.set_free(bC, t_cnt)
            t_bec = S.op("vector", lambda E: E.tensor_add(becr[0:1, 0, :], becr[0:1, 0, :], becr[0:1, 1, :]), waits=[t_cnt, tgB], inc="v")
            becr_tok[0] = t_bec
            t_ovf = S.op("vector", lambda E: E.tensor_tensor(gs2[:, 2, :], gs2[:, 1, :], thr_sb[:], ALU.is_ge), waits=[t_slot], inc="v")
            t_ns = S.op("vector", lambda E: E.tensor_scalar(gs2[:, 3, :], gsmall[:, 9, 0:NE], -1.0, 1.0, ALU.mult, ALU.add), inc="v")
            t_inv = S.op("vector", lambda E: E.tensor_add(gs2[:, 3, :], gs2[:, 3, :], gs2[:, 2, :]), waits=[t_ovf, t_ns], inc="v")
            t_sl2 = S.op("vector", lambda E: E.scalar_tensor_tensor(gs2[:, 4, :], gs2[:, 3, :], BIGF, gs2[:, 1, :], ALU.mult, ALU.add), waits=[t_inv], inc="v")
            t_last = t_sl2
            for k in range(4):
                t_mk = S.op("vector", lambda E, k=k: E.tensor_scalar(gs2[:, 5, :], gsmall[:, 7, 0:NE], gsmall[:, 5, 2 + k:3 + k], None, ALU.is_equal), waits=[t_last, tI] if k == 0 else [t_last], inc="v")
                t_sv = S.op("vector", lambda E: E.tensor_mul(gs2[:, 0, :], gs2[:, 5, :], gs2[:, 4, :]), waits=[t_mk], inc="v")
                t_sk = S.op("vector", lambda E, tj=tj, k=k: E.reduce_sum(skf[:, tj, k:k + 1], gs2[:, 0, :], axis=AX.X), waits=[t_sv], inc="v")
                t_cv = S.op("vector", lambda E: E.tensor_mul(gs2[:, 0, :], gs2[:, 5, :], gsmall[:, 10, 0:NE]), waits=[t_sk], inc="v")
                t_last = S.op("vector", lambda E, tj=tj, k=k: E.reduce_sum(cwf[:, tj, k:k + 1], gs2[:, 0, :], axis=AX.X), waits=[t_cv], inc="v")
            t_ski = S.op("vector", lambda E, tj=tj: E.tensor_copy(ski[:, tj, :], skf[:, tj, :]), waits=[t_last], inc="v")
            t_aux = None
            for k in range(4):
                S.op("vector", lambda E, tj=tj, k=k: E.tensor_copy(auxsrc[:, tj, k, 0:1], cwf[:, tj, k:k + 1]),
                     waits=[t_last, t_auxi] if k == 0 else [], inc="v")
                t_aux = S.op("vector", lambda E, tj=tj, k=k: E.tensor_copy(auxsrc[:, tj, k, 1 + k:2 + k], tokidf_sb[:, tj:tj + 1]),
                             inc="v")
            sp2 = tj % 2
            t_lev = None
            for g2 in range(2):
                tt2 = None
                for k8 in range(8):
                    kc = g2 * 8 + k8
                    w = ([pt7_free[0]] if pt7_free[0] is not None else []) if k8 == 0 else []
                    tt2 = S.op("tensor",
                               lambda E, kc=kc, k8=k8, tj=tj:
                               E.transpose(PT7[:, k8 * 128:(k8 + 1) * 128], ln_fm[:, kc, tj * 128:(tj + 1) * 128], idb_sb[:]),
                               waits=w, inc=("p" if k8 == 7 else None))
                wl = [lntm_free[sp2]] if (g2 == 0 and lntm_free[sp2] is not None) else []
                t_lev = S.op("vector",
                             lambda E, g2=g2, sp2=sp2:
                             E.tensor_copy(lntm2[:, sp2, g2 * 1024:(g2 + 1) * 1024], PT7[:]),
                             waits=[tt2] + wl, inc="v")
                pt7_free[0] = t_lev
            if pending_disp[0] is not None:
                pending_disp[0]()

            def mk_disp(tj=tj, sp2=sp2, t_ski=t_ski, t_aux=t_aux, t_lev=t_lev):
                def emit_():
                    t_xs = None
                    for k in range(4):
                        w = [t_ski, t_lev, t_big] if k == 0 else []
                        t_xs = S.op("gpsimd",
                                    lambda E, tj=tj, k=k, sp2=sp2:
                                    E.indirect_dma_start(disp_d[:, :], IndirectOffsetOnAxis(ski[:, tj, k:k + 1], 0),
                                                         lntm2[:, sp2, :], None,
                                                         bounds_check=r_bdisp, oob_is_err=False),
                                    waits=w, inc="g4", inc_k=16)
                    lntm_free[sp2] = t_xs
                    for k in range(4):
                        S.op("gpsimd",
                             lambda E, tj=tj, k=k:
                             E.indirect_dma_start(aux_d[:, :], IndirectOffsetOnAxis(ski[:, tj, k:k + 1], 0),
                                                  auxsrc[:, tj, k, :], None,
                                                  bounds_check=r_bdisp, oob_is_err=False),
                             waits=[t_aux, t_ski] if k == 0 else [], inc="g4", inc_k=16)
                return emit_
            pending_disp[0] = mk_disp()
            fout["tg"] = tg
            yield
          if pending_disp[0] is not None:
              pending_disp[0]()
              pending_disp[0] = None
          fout["tg"] = tg

        if sparse:
            gs_, gt_ = gen_shared(), gen_tj()
            done_s = done_t = False
            while not (done_s and done_t):
                if not done_s:
                    done_s = next(gs_, "END") == "END"
                if not done_t:
                    done_t = next(gt_, "END") == "END"
        else:
            for _ in gen_tj():
                pass
        tg = fout["tg"]
        if sparse:
            t_disp_pre = ("g4", S.cnt["g4"])
            t_xg0 = None
            for kc in range(KC):
                t_xg0 = S.op("sync",
                             lambda E, kc=kc:
                             E.dma_start(xg0[:, kc, :], disp_d[0:CAP, kc * 128:(kc + 1) * 128], transpose=True),
                             waits=[t_disp_pre] if kc == 0 else [], inc="x0", inc_k=16)
        t_cw = ("v", S.cnt["v"])
        if debug:
            S.op("sync", lambda E: E.dma_start(dbg_cw[:, :], cwT[:, :]), waits=[t_cw], inc="i", inc_k=16)
        if not sparse:
            consume_w(p_gw, tg)
        if sparse:
            t_disp_all = ("g4", S.cnt["g4"])
            S.flush()
            sharedS.close()
            fS.close()
            abufS.close()
            lnfS.close()

            expS = ExitStack()
            xg = [xg0, expS.enter_context(nc.sbuf_tensor("xg1", [128, KC, CAP], BF16))]
            asp = expS.enter_context(nc.sbuf_tensor("asp", [128, EIC, CAP], BF16))
            y_tm = [expS.enter_context(nc.sbuf_tensor(f"ytm{i}", [128, D], BF16)) for i in range(6)]
            auxt = [expS.enter_context(nc.sbuf_tensor(f"auxt{i}", [128, 3, 8], F32)) for i in range(2)]
            idxt = [expS.enter_context(nc.sbuf_tensor(f"idxt{i}", [128, 3, 4], I32)) for i in range(2)]
            ws.append(expS.enter_context(nc.sbuf_tensor("ws2", [128, KC, 512], BF16)))
            ws.append(expS.enter_context(nc.sbuf_tensor("ws3", [128, KC, 512], BF16)))
            ws_mod[0] = 4
            rot.t.append(PT6)
            rot.free_tok.append(pt6_free[0])
            NTIL = CAP // 128
            S.phase()
            t_z0 = S.op("vector", lambda E: E.memset(y_tm[0][:], 0.0), inc="v")
            t_zero = None
            for k in range(4):
                for j in range(NT):
                    t_zero = S.op("sync",
                                  lambda E, k=k, j=j: E.dma_start(acc_d[k][j * 128:(j + 1) * 128, :], y_tm[0][:]),
                                  waits=[t_z0] if (k == 0 and j == 0) else [], inc="z", inc_k=16)
            xg_free = [None, None]
            cw_free = [None, None]
            y_free = [None] * 6
            abuf_free2 = [None]
            pending_scat = None
            for e in range(n_experts):
                pe = e % 2
                if e == 0:
                    t_xg = t_xg0
                else:
                    t_xg = None
                    for kc in range(KC):
                        w = ([xg_free[pe], t_disp_all] if kc == 0 else [])
                        t_xg = S.op("sync",
                                    lambda E, e=e, kc=kc, pe=pe:
                                    E.dma_start(xg[pe][:, kc, :], disp_d[e * CAP:(e + 1) * CAP, kc * 128:(kc + 1) * 128], transpose=True),
                                    waits=w, inc=f"x{pe}", inc_k=16)
                t_axl = S.op("sync",
                             lambda E, e=e, pe=pe:
                             E.dma_start(auxt[pe][:, :, :], aux_d[e * CAP:(e + 1) * CAP, :].rearrange("(i p) o -> p i o", p=128)),
                             waits=[cw_free[pe], t_disp_all] if cw_free[pe] is not None else [t_disp_all],
                             inc=f"c{pe}", inc_k=16)
                t_idx = S.op("vector", lambda E, pe=pe: E.tensor_copy(idxt[pe][:], auxt[pe][:, :, 1:5]), waits=[t_axl], inc="v")
                tg_last = None
                for cb in range(6):
                    ncol = 512 if cb < 5 else 256
                    npair = 2 if cb < 5 else 1
                    p, t_rdy = stream_w(up_d[e, :, cb * 512:cb * 512 + ncol].rearrange("(kc p) n -> p kc n", p=128), KC, ncol)
                    for pr in range(npair):
                        bg = rot.next()
                        bu = rot.next()
                        wg_ = [t_rdy] if pr == 0 else []
                        if cb == 0 and pr == 0:
                            wg_ = wg_ + [t_xg, abuf_free2[0]]
                        mms_g = [(rot.t[bg][:, 0:CAP], ws[p][:, kc, pr * 256:pr * 256 + 128], xg[pe][:, kc, :]) for kc in range(KC)]
                        mms_u = [(rot.t[bu][:, 0:CAP], ws[p][:, kc, pr * 256 + 128:pr * 256 + 256], xg[pe][:, kc, :]) for kc in range(KC)]
                        tgg = S.mm_group(mms_g, waits=wg_ + rot.waits(bg))
                        tgu = S.mm_group(mms_u, waits=rot.waits(bu))
                        tg_last = tgu
                        ic = cb * 2 + pr
                        t_sg = S.op("scalar",
                                    lambda E, bg=bg: E.activation(scr[0][:, 0:CAP], rot.t[bg][:, 0:CAP], AF.Silu),
                                    waits=[tgg] + ([scr0_war[0]] if scr0_war[0] is not None else []), inc="a")
                        t_a = S.op("vector",
                                   lambda E, ic=ic, bu=bu: E.tensor_mul(asp[:, ic, :], scr[0][:, 0:CAP], rot.t[bu][:, 0:CAP]),
                                   waits=[t_sg, tgu], inc="v")
                        scr0_war[0] = t_a
                        rot.set_free(bg, t_sg)
                        rot.set_free(bu, t_a)
                    consume_w(p, tg_last)
                xg_free[pe] = tg_last
                t_a_all = ("v", S.cnt["v"])
                t_ye = [None, None, None]
                for cb in range(4):
                    p, t_rdy = stream_w(dn_d[e, :, cb * 512:(cb + 1) * 512].rearrange("(kc p) n -> p kc n", p=128), EIC, 512)
                    tg = None
                    for i in range(NTIL):
                        b = rot.next()
                        mms = [(rot.t[b][:, :], asp[:, kc, i * 128:(i + 1) * 128], ws[p][:, kc, :])
                               for kc in range(EIC)]
                        w = ([t_rdy, t_a_all] if i == 0 else []) + rot.waits(b)
                        if cb == 0 and i == 0:
                            w = w + [t_idx]
                        tg = S.mm_group(mms, waits=w)
                        yt = pe * NTIL + i
                        wy = [y_free[yt]] if (cb == 0 and y_free[yt] is not None) else []
                        t_ye[i] = S.op("vector",
                                       lambda E, i=i, cb=cb, b=b, pe=pe, yt=yt:
                                       E.tensor_scalar(y_tm[yt][:, cb * 512:(cb + 1) * 512], rot.t[b][:], auxt[pe][:, i, 0:1], None, ALU.mult),
                                       waits=[tg] + wy, inc="v")
                        rot.set_free(b, t_ye[i])
                    consume_w(p, tg)
                abuf_free2[0] = tg
                if pending_scat is not None:
                    pending_scat()
                    pending_scat = None

                def mk_scat(e=e, pe=pe, t_ye=tuple(t_ye), t_idx=t_idx, first=(e == 0)):
                    def emit_():
                        for i in range(NTIL):
                            t_s = None
                            for k in range(4):
                                w = [t_ye[i], t_idx]
                                if first and i == 0 and k == 0:
                                    w = w + [t_zero]
                                t_s = S.op("gpsimd",
                                           lambda E, i=i, k=k, pe=pe:
                                           E.indirect_dma_start(acc_d[k][:, :], IndirectOffsetOnAxis(idxt[pe][:, i, k:k + 1], 0),
                                                                y_tm[pe * NTIL + i][:, :], None,
                                                                bounds_check=r_bacc, oob_is_err=False),
                                           waits=w, inc="g3", inc_k=16)
                            y_free[pe * NTIL + i] = t_s
                        cw_free[pe] = t_s
                    return emit_
                pending_scat = mk_scat()
            if pending_scat is not None:
                pending_scat()
                pending_scat = None
            t_scat_all = ("g3", S.cnt["g3"])
            S.flush()
            expS.close()
            ws.pop()
            ws.pop()
            ws_mod[0] = 2
            rot.t.pop()
            pt6_free[0] = rot.free_tok.pop()
            rot.i = rot.i % 6

            mergeS = ExitStack()
            macc = [mergeS.enter_context(nc.sbuf_tensor(f"macc{i}", [128, D], BF16)) for i in range(2)]
            msum = mergeS.enter_context(nc.sbuf_tensor("msum", [128, D], F32))
            S.phase()
            m_free = [None, None]
            msum_pe = [None]
            for tj in range(NT):
                tl = [None] * 4
                for k in range(4):
                    mp = k % 2
                    w = ([t_scat_all] if tj == 0 and k < 2 else [])
                    if m_free[mp] is not None:
                        w = w + [m_free[mp]]
                    tl[k] = S.op("sync",
                                 lambda E, k=k, tj=tj, mp=mp: E.dma_start(macc[mp][:], acc_d[k][tj * 128:(tj + 1) * 128, :]),
                                 waits=w, inc=f"c{mp}", inc_k=16)
                    if k == 1:
                        s1w = [tl[0], tl[1]]
                        if msum_pe[0] is not None:
                            s1w = s1w + [msum_pe[0]]
                        s1 = S.op("vector", lambda E: E.tensor_add(msum[:], macc[0][:], macc[1][:]), waits=s1w, inc="v")
                        m_free[0] = s1
                        m_free[1] = s1
                s2 = S.op("vector", lambda E: E.tensor_add(msum[:], msum[:], macc[0][:]), waits=[tl[2], s1], inc="v")
                s3 = S.op("vector", lambda E: E.tensor_add(msum[:], msum[:], macc[1][:]), waits=[tl[3], s2], inc="v")
                m_free[0] = s2
                m_free[1] = s3
                for g in range(4):
                    tt = None
                    for kc4 in range(4):
                        kc = g * 4 + kc4
                        w = ([s3] + ([pt6_free[0]] if pt6_free[0] is not None else [])) if kc4 == 0 else []
                        tt = S.op("tensor",
                                  lambda E, kc=kc, kc4=kc4, tj=tj:
                                  E.transpose(PT6[:, kc4 * 128:(kc4 + 1) * 128], msum[:, kc * 128:(kc + 1) * 128], idf_sb[:]),
                                  waits=w, inc=("p" if kc4 == 3 else None))
                    t_ev = S.op("vector",
                                lambda E, g=g, tj=tj:
                                E.tensor_add(x_fm[:, g * 4:(g + 1) * 4, tj * 128:(tj + 1) * 128],
                                             x_fm[:, g * 4:(g + 1) * 4, tj * 128:(tj + 1) * 128],
                                             PT6[:].rearrange("p (c n) -> p c n", c=4)),
                                waits=[tt], inc="v")
                    pt6_free[0] = t_ev
                msum_pe[0] = tt
            t_final = ("v", S.cnt["v"])
            S.flush()
            mergeS.close()
        else:
            S.flush()
            S.phase()
            cwb_ready = [None, None]
            last_bcast = [None]
            abuf_free = [None]
            scr0_war = [None]
            t_ev = None

            def emit_cwb(e):
                par = e % 2
                t_row = S.op("gpsimd", lambda E, e=e: E.dma_start(crow[0:1, :], cwT[e:e + 1, :]),
                             waits=[t_cw, cwb_ready[par], last_bcast[0]],
                             inc="g2", inc_k=16)
                toks = []
                for half in range(2):
                    b = rot.next()
                    tgb = S.mm_group([(rot_t[b][:, :], ones1[:], crow[0:1, half * 512:(half + 1) * 512])],
                                     waits=[t_row] + rot.waits(b))
                    last_bcast[0] = tgb
                    t_evb = S.op("vector",
                                 lambda E, par=par, half=half, b=b:
                                 E.tensor_copy(cwb[par][:, half * 512:(half + 1) * 512], rot_t[b][:]),
                                 waits=[tgb], inc="v")
                    rot.set_free(b, t_evb)
                    toks.append(t_evb)
                cwb_ready[par] = toks[-1]
                return toks[-1]

            for e in range(n_experts):
                t_cwb = emit_cwb(e)
                par = e % 2
                tg_last = None
                for cb in range(6):
                    ncol = 512 if cb < 5 else 256
                    npair = 2 if cb < 5 else 1
                    p, t_rdy = stream_w(up_d[e, :, cb * 512:cb * 512 + ncol].rearrange("(kc p) n -> p kc n", p=128), KC, ncol)
                    for pr in range(npair):
                        for half in range(2):
                            bg = rot.next()
                            bu = rot.next()
                            hsl = slice(half * 512, (half + 1) * 512)
                            wg_ = [t_rdy] if (pr == 0 and half == 0) else []
                            mms_g = [(rot_t[bg][:, :], ws[p][:, kc, pr * 256:pr * 256 + 128], ln_fm[:, kc, hsl]) for kc in range(KC)]
                            mms_u = [(rot_t[bu][:, :], ws[p][:, kc, pr * 256 + 128:pr * 256 + 256], ln_fm[:, kc, hsl]) for kc in range(KC)]
                            tgg = S.mm_group(mms_g, waits=wg_ + rot.waits(bg))
                            tgu = S.mm_group(mms_u, waits=rot.waits(bu))
                            tg_last = tgu
                            ic = cb * 2 + pr
                            aw = [abuf_free[0]] if (ic == 0 and half == 0) else []
                            aw = aw + ([scr0_war[0]] if scr0_war[0] is not None else [])
                            t_sg = S.op("scalar",
                                        lambda E, bg=bg: E.activation(scr[0][:], rot_t[bg][:], AF.Silu),
                                        waits=[tgg] + aw, inc="a")
                            t_t = S.op("vector",
                                       lambda E, bu=bu: E.tensor_mul(scr[1][:], scr[0][:], rot_t[bu][:]),
                                       waits=[t_sg, tgu] + ([t_cwb] if (ic == 0 and half == 0) else []), inc="v")
                            scr0_war[0] = t_t
                            t_a = S.op("vector",
                                       lambda E, ic=ic, half=half, par=par, hsl=hsl:
                                       E.tensor_mul(abuf[:, 2 * ic + half, :], scr[1][:], cwb[par][:, hsl]),
                                       inc="v")
                            rot.set_free(bg, t_a)
                            rot.set_free(bu, t_a)
                    consume_w(p, tg_last)
                t_a_all = ("v", S.cnt["v"])
                for cb in range(4):
                    p, t_rdy = stream_w(dn_d[e, :, cb * 512:(cb + 1) * 512].rearrange("(kc p) n -> p kc n", p=128), EIC, 512)
                    tg = None
                    for c in range(4):
                        for half in range(2):
                            b = rot.next()
                            hsl = slice(half * 512, (half + 1) * 512)
                            mms = [(rot_t[b][:, :], ws[p][:, kc, c * 128:(c + 1) * 128],
                                    abuf[:, 2 * kc + half, :]) for kc in range(EIC)]
                            w = ([t_rdy, t_a_all] if (c == 0 and half == 0) else []) + rot.waits(b)
                            tg = S.mm_group(mms, waits=w)
                            t_ev = S.op("vector",
                                        lambda E, cb=cb, c=c, b=b, hsl=hsl:
                                        E.tensor_add(x_fm[:, cb * 4 + c, hsl], x_fm[:, cb * 4 + c, hsl], rot_t[b][:]),
                                        waits=[tg], inc="v")
                            rot.set_free(b, t_ev)
                    consume_w(p, tg)
                abuf_free[0] = tg

            for half in range(2):
                hsl = slice(half * 512, (half + 1) * 512)
                tg_last = None
                for cb in range(11):
                    p, t_rdy = stream_w(sup_d[:, cb * 512:(cb + 1) * 512].rearrange("(kc p) n -> p kc n", p=128), KC, 512)
                    for pr in range(2):
                        bg = rot.next()
                        bu = rot.next()
                        mms_g = [(rot_t[bg][:, :], ws[p][:, kc, pr * 256:pr * 256 + 128], ln_fm[:, kc, hsl]) for kc in range(KC)]
                        mms_u = [(rot_t[bu][:, :], ws[p][:, kc, pr * 256 + 128:pr * 256 + 256], ln_fm[:, kc, hsl]) for kc in range(KC)]
                        wg_ = [t_rdy] if pr == 0 else []
                        if cb == 0 and pr == 0:
                            wg_ = wg_ + [abuf_free[0]]
                        ic = cb * 2 + pr
                        tgg = S.mm_group(mms_g, waits=wg_ + rot.waits(bg))
                        tgu = S.mm_group(mms_u, waits=rot.waits(bu))
                        tg_last = tgu
                        t_sg = S.op("scalar",
                                    lambda E, bg=bg: E.activation(scr[0][:], rot_t[bg][:], AF.Silu),
                                    waits=[tgg] + ([scr0_war[0]] if scr0_war[0] is not None else []), inc="a")
                        t_a = S.op("vector",
                                   lambda E, ic=ic, bu=bu: E.tensor_mul(abuf[:, ic, :], scr[0][:], rot_t[bu][:]),
                                   waits=[t_sg, tgu], inc="v")
                        scr0_war[0] = t_a
                        rot.set_free(bg, t_a)
                        rot.set_free(bu, t_a)
                    consume_w(p, tg_last)
                t_as = ("v", S.cnt["v"])
                tg = None
                for cb in range(8):
                    p, t_rdy = stream_w2(sdn_d[:, cb * 256:(cb + 1) * 256].rearrange("(kc p) n -> p kc n", p=128), SIC, 256)
                    wv22 = ws[p][:].rearrange("p a b -> p (a b)").rearrange("p (k m) -> p k m", m=256)
                    for c in range(2):
                        b = rot.next()
                        mms = [(rot_t[b][:, :], wv22[:, kc, c * 128:(c + 1) * 128], abuf[:, kc, :])
                               for kc in range(SIC)]
                        w = ([t_rdy, t_as] if (c == 0 and cb == 0) else ([t_rdy] if c == 0 else [])) + rot.waits(b)
                        tg = S.mm_group(mms, waits=w)
                        t_ev = S.op("vector",
                                    lambda E, cb=cb, c=c, b=b, hsl=hsl:
                                    E.tensor_add(x_fm[:, cb * 2 + c, hsl], x_fm[:, cb * 2 + c, hsl], rot_t[b][:]),
                                    waits=[tg], inc="v")
                        rot.set_free(b, t_ev)
                    consume_w(p, tg)
                abuf_free[0] = tg
            t_final = ("v", S.cnt["v"])
            S.flush()
            abuf0S.close()
            abufS.close()
            lnfS.close()

        S.phase()
        outS = ExitStack()
        xout = [outS.enter_context(nc.sbuf_tensor(f"xout{i}", [128, D], F32)) for i in range(2)]
        xout_free = [None, None]
        for j in range(NT):
            sp = j % 2
            last_ev = None
            for g in range(4):
                tt = None
                for kc4 in range(4):
                    kc = g * 4 + kc4
                    w = ([pt6_free[0]] if pt6_free[0] is not None else []) if kc4 == 0 else []
                    if j == 0 and g == 0 and kc4 == 0:
                        w = w + [t_final]
                    tt = S.op("tensor",
                              lambda E, j=j, kc=kc, kc4=kc4:
                              E.transpose(PT6[:, kc4 * 128:(kc4 + 1) * 128], x_fm[:, kc, j * 128:(j + 1) * 128], idf_sb[:]),
                              waits=w, inc=("p" if kc4 == 3 else None))
                wx = [xout_free[sp]] if (g == 0 and xout_free[sp] is not None) else []
                last_ev = S.op("vector",
                               lambda E, sp=sp, g=g: E.tensor_copy(xout[sp][:, g * 512:(g + 1) * 512], PT6[:]),
                               waits=[tt] + wx, inc="v")
                pt6_free[0] = last_ev
            t_st = S.op("sync",
                        lambda E, j=j, sp=sp: E.dma_start(out_d[j * 128:(j + 1) * 128, :], xout[sp][:]),
                        waits=[last_ev], inc=f"o{sp}", inc_k=16)
            xout_free[sp] = t_st
        S.op("sync", lambda E: E.nop(), waits=[("o0", S.cnt["o0"]), ("o1", S.cnt["o1"])])
        S.flush()
        outS.close()
    return nc


def _host_prep(inputs):
    x = np.ascontiguousarray(np.asarray(inputs["L0_attn_x"], dtype=np.float32).reshape(8192, D))
    latq = np.asarray(inputs["L0_attn_latent_queries"], dtype=np.float32)[0]
    Wk = np.asarray(inputs["L0_attn_Wk"], dtype=np.float32)
    Wv = np.asarray(inputs["L0_attn_Wv"], dtype=np.float32)
    Wg = np.asarray(inputs["L0_attn_Wg"], dtype=np.float32)
    Wp = np.asarray(inputs["L0_attn_Wp"], dtype=np.float32)
    gw = np.asarray(inputs["L0_ffn_gate_w"], dtype=np.float32)
    upw = np.asarray(inputs["L0_ffn_expert_up_w"], dtype=np.float32)
    dnw = np.asarray(inputs["L0_ffn_expert_down_w"], dtype=np.float32)
    sup = np.asarray(inputs["L0_ffn_shared_up_w"], dtype=np.float32)
    sdn = np.asarray(inputs["L0_ffn_shared_down_w"], dtype=np.float32)

    dperm = np.concatenate([np.arange(0, HD, 2), np.arange(1, HD, 2)])
    colperm = np.concatenate([h * HD + dperm for h in range(NH)])
    Wk_p = np.ascontiguousarray(Wk[:, colperm])
    q_p = np.ascontiguousarray(latq.transpose(2, 1, 0)[dperm])

    inv = 1.0 / (10000.0 ** (np.arange(64) * 2.0 / HD))
    ang = np.arange(128)[None, :] * inv[:, None]
    cosr = np.ascontiguousarray(np.cos(ang).astype(np.float32))
    sinr = np.ascontiguousarray(np.sin(ang).astype(np.float32))
    maskr = np.ascontiguousarray(
        np.where(np.arange(128)[None, :] <= np.arange(NL)[:, None], 0.0, -1e9).astype(np.float32))

    idb = np.eye(128, dtype=ml_dtypes.bfloat16)
    idf = np.eye(128, dtype=np.float32)

    def pairperm(wT, ich):
        cols = []
        for i in range(ich):
            cols.append(wT[:, i * 128:(i + 1) * 128])
            cols.append(wT[:, (ich + i) * 128:(ich + i + 1) * 128])
        return np.ascontiguousarray(np.concatenate(cols, axis=1))

    up_p = np.ascontiguousarray(np.stack([pairperm(upw[e].T, EIC) for e in range(NE)]))
    dn_p = np.ascontiguousarray(np.stack([np.ascontiguousarray(dnw[e].T) for e in range(NE)]))
    sup_p = pairperm(np.ascontiguousarray(sup.T), SIC)
    sdn_p = np.ascontiguousarray(sdn.T)
    gw_t = np.ascontiguousarray(gw.T)

    ltri = np.ascontiguousarray(np.triu(np.ones((128, 128)), 1).astype(ml_dtypes.bfloat16))
    ecrow = np.ascontiguousarray((np.arange(NE) * CAP).astype(np.float32).reshape(1, NE))
    thr = np.ascontiguousarray(np.broadcast_to((np.arange(NE) * CAP + CAP).astype(np.float32), (128, NE)).copy())
    tokidf = np.ascontiguousarray((np.arange(NT)[None, :] * 128 + np.arange(128)[:, None]).astype(np.float32))
    bigf = np.full((128, 384), BIGF, dtype=np.float32)

    shared = {
        "xp": np.ascontiguousarray(x[0:128]),
        "ltri": ltri, "ecrow": ecrow, "thr": thr, "tokidf": tokidf, "bigf": bigf,
        "wk": Wk_p, "wv": Wv, "wg": Wg, "wp": Wp,
        "q": q_p, "cosr": cosr, "sinr": sinr, "maskr": maskr,
        "idb": idb, "idf": idf, "gwt": gw_t,
        "upw": up_p, "dnw": dn_p, "supw": sup_p, "sdnw": sdn_p,
    }
    in_maps = []
    for c in range(NCORE):
        m = dict(shared)
        m["x"] = np.ascontiguousarray(x[c * TOK:(c + 1) * TOK])
        in_maps.append(m)
    return in_maps


def kernel(**inputs):
    in_maps = _host_prep(inputs)
    nc = build()
    res = run_bass_kernel_spmd(nc, in_maps, core_ids=list(range(NCORE)))
    out = np.concatenate([np.asarray(r["out"]) for r in res.results], axis=0)
    return out.reshape(1, 8192, D).astype(np.float32)
